# version 01
# baseline (speedup 1.0000x reference)
import contextlib
import numpy as np
import ml_dtypes
import concourse.bass as bass
import concourse.mybir as mybir
from concourse.bass_utils import run_bass_kernel_spmd

F32 = mybir.dt.float32
BF16 = mybir.dt.bfloat16
AF = mybir.ActivationFunctionType
ALU = mybir.AluOpType
AX = mybir.AxisListType
BFD = ml_dtypes.bfloat16

B, T, D = 2, 4096, 2048
DH = 128
NFF = 8192
KTOP = 819
SCALE = float(DH ** -0.5)
EPS = 1e-5
NEG = -1e30
NT_OWN = 8
NTILE_SEQ = 32
DBLK = 16
UCH = 64
GRP = 256

CW_C1 = float(np.float32(int(np.float64(2 * np.pi) * 2**9) / 2**9))
CW_C2 = float(np.float32(int((np.float64(2 * np.pi) - CW_C1) * 2**22) / 2**22))
CW_C3 = float(np.float32(np.float64(2 * np.pi) - CW_C1 - CW_C2))
HPI = float(np.float32(np.pi / 2))


class Rec:
    ENGS = ("t", "v", "s", "g", "y")

    def __init__(self, nc, ctx):
        self.nc = nc
        self.ctx = ctx
        self.sem = {e: ctx.enter_context(nc.semaphore(f"es_{e}")) for e in self.ENGS}
        self.cnt = {e: 0 for e in self.ENGS}
        self.ops = {e: [] for e in self.ENGS}
        self.last_w = {}
        self.readers = {}
        self.water = {e: {} for e in self.ENGS}
        self.max_wait = {}
        self.dsem = {}
        self.dcnt = {}
        self._dsem_id = 0

    def _get_dsem(self, key):
        if key not in self.dsem:
            self._dsem_id += 1
            self.dsem[key] = self.ctx.enter_context(
                self.nc.semaphore(f"dq{self._dsem_id}"))
            self.dcnt[key] = 0
        return self.dsem[key]

    def op(self, e, fn, reads=(), writes=(), dma=False, mark=True):
        need = []
        for r in reads:
            w = self.last_w.get(r)
            if w is not None:
                need.append(w)
        for w_ in writes:
            w = self.last_w.get(w_)
            if w is not None:
                need.append(w)
            rd = self.readers.get(w_)
            if rd:
                for k, c in rd.items():
                    need.append((k[0], k[1], c))
        deps = {}
        for kind, peng, c in need:
            if kind == "e" and peng == e and e == "t" and not dma:
                continue
            key = (kind, peng)
            if self.water[e].get(key, 0) >= c:
                continue
            deps[key] = max(deps.get(key, 0), c)
        if dma:
            semkey = writes[0] if writes else f"__pool_{e}"
            self._get_dsem(semkey)
            self.dcnt[semkey] += 1
            mykind, myval = semkey, 16 * self.dcnt[semkey]
        elif mark:
            self.cnt[e] += 1
            mykind, myval = "e", self.cnt[e]
        else:
            mykind, myval = "e", self.cnt[e] + 1
        waits = []
        for (kind, peng), c in deps.items():
            sem = self.sem[peng] if kind == "e" else self.dsem[kind]
            waits.append((sem, c))
            self.water[e][(kind, peng)] = c
            self.max_wait[(kind, peng)] = max(self.max_wait.get((kind, peng), 0), c)
        msem = self.dsem[mykind] if dma else (self.sem[e] if mark else None)
        minc = 16 if dma else 1

        def closure(eng, waits=waits, fn=fn, msem=msem, minc=minc):
            for sem, c in waits:
                eng.wait_ge(sem, c)
            instr = fn(eng)
            if msem is not None:
                instr.then_inc(msem, minc)
        self.ops[e].append(closure)
        myk = (mykind, e)
        for r in reads:
            d = self.readers.setdefault(r, {})
            d[myk] = max(d.get(myk, 0), myval)
        for w_ in writes:
            self.last_w[w_] = (mykind, e, myval)
            self.readers[w_] = {}

    def wait_all(self, e):
        waits = []
        for pe in self.ENGS:
            if pe != e and self.cnt[pe] and self.water[e].get(("e", pe), 0) < self.cnt[pe]:
                waits.append((self.sem[pe], self.cnt[pe]))
                self.water[e][("e", pe)] = self.cnt[pe]
                self.max_wait[("e", pe)] = max(self.max_wait.get(("e", pe), 0), self.cnt[pe])
        for sk, n in self.dcnt.items():
            if n and self.water[e].get((sk, "*"), 0) < 16 * n:
                waits.append((self.dsem[sk], 16 * n))
                self.water[e][(sk, "*")] = 16 * n
                self.max_wait[(sk, "*")] = max(self.max_wait.get((sk, "*"), 0), 16 * n)

        def closure(eng, waits=waits):
            for sem, c in waits:
                eng.wait_ge(sem, c)
        self.ops[e].append(closure)

    def barrier(self):
        for e in self.ENGS:
            self.wait_all(e)

    def check(self):
        for (kind, peng), v in self.max_wait.items():
            if kind == "e":
                avail = self.cnt[peng]
            else:
                avail = 16 * self.dcnt.get(kind, 0)
            assert v <= avail, f"wait {v} on {kind}/{peng}, only {avail} will arrive"

    def finalize(self, block):
        self.check()
        for e, meth in (("y", block.sync), ("t", block.tensor), ("v", block.vector),
                        ("s", block.scalar), ("g", block.gpsimd)):
            ops = self.ops[e]

            def body(eng, ops=ops):
                for f in ops:
                    f(eng)
            meth(body)


def host_tables(r):
    TWO_PI = np.float64(2 * np.pi)
    inv_freq = (1.0 / (1e6 ** (np.arange(DH // 2) / (DH // 2)))).astype(np.float64)
    f128 = np.concatenate([inv_freq, inv_freq])
    fhi = f128.astype(np.float32)
    flo = (f128 - fhi.astype(np.float64)).astype(np.float32)
    own_rows = np.concatenate([np.arange(128 * (4 * j + r), 128 * (4 * j + r) + 128)
                               for j in range(NT_OWN)])
    t_full = np.arange(T, dtype=np.float32).reshape(1, T)
    t_own = own_rows.astype(np.float32).reshape(1, 1024)

    def ktabs(tv):
        ang = np.outer(f128, tv.astype(np.float64).ravel())
        return (np.rint(ang / TWO_PI).astype(np.float32),
                np.rint((ang + np.pi / 2) / TWO_PI).astype(np.float32))
    ksf, kcf = ktabs(t_full)
    kso, kco = ktabs(t_own)
    i = np.arange(128).reshape(128, 1)
    c = np.arange(512).reshape(1, 512)
    cmask = np.where(c <= 128 * r + i, 0.0, NEG).astype(np.float32)
    return dict(fhi=fhi.reshape(1, 128), flo=flo.reshape(1, 128),
                tf=t_full, to=t_own, ksf=ksf, kcf=kcf, kso=kso, kco=kco,
                cmask=cmask, own_rows=own_rows,
                ident=np.eye(128, dtype=BFD),
                ones=np.ones((1, 128), dtype=np.float32),
                iota16=np.broadcast_to(np.arange(1.0, 17.0, dtype=np.float32),
                                       (128, 16)).copy())


def build_program(ctx, dbg=False):
    nc = bass.Bass("TRN2", target_bir_lowering=False)
    E = ctx.enter_context

    def dram(n, shp, dt, kind="ExternalInput"):
        return nc.dram_tensor(n, shp, dt, kind=kind)
    d_xf = dram("xf", [T, D], BF16)
    d_xo = dram("xo", [1024, D], BF16)
    d_wq = dram("wq", [D, D], BF16)
    d_wk = dram("wk", [D, 512], BF16)
    d_wv = dram("wv", [D, 512], BF16)
    d_wo = dram("wo", [D, D], BF16)
    d_bqc = dram("bqc", [128, 16], BF16)
    d_bkc = dram("bkc", [128, 4], BF16)
    d_bvr = dram("bvr", [1, 512], BF16)
    d_wsyn = dram("wsyn", [NFF, D], F32)
    d_wout = dram("wout", [D, NFF], F32)
    d_bsyn = dram("bsyn", [1, NFF], F32)
    d_bout = dram("bout", [1, D], F32)
    d_gam = dram("gam", [1, D], F32)
    d_bet = dram("bet", [1, D], F32)
    d_fhi = dram("cfhi", [1, 128], F32)
    d_flo = dram("cflo", [1, 128], F32)
    d_tf = dram("ctf", [1, T], F32)
    d_to = dram("cto", [1, 1024], F32)
    d_ksf = dram("cksf", [128, T], F32)
    d_kcf = dram("ckcf", [128, T], F32)
    d_kso = dram("ckso", [128, 1024], F32)
    d_kco = dram("ckco", [128, 1024], F32)
    d_cm = dram("ccm", [128, 512], F32)
    d_id = dram("cid", [128, 128], BF16)
    d_ones = dram("cones", [1, 128], F32)
    d_iota = dram("ciota", [128, 16], F32)
    d_y = dram("y", [1024, D], F32, kind="ExternalOutput")
    d_tsyn = dram("scr_tsyn", [DBLK, 128, NFF], BF16, kind="Internal")
    d_tout = dram("scr_tout", [UCH, 128, D], BF16, kind="Internal")
    d_x1 = dram("scr_x1", [1024, D], BF16, kind="Internal")
    dbg_t = {}
    if dbg:
        for name, shp, dt in [("dxln", [T, D], BF16), ("dv", [T, 512], BF16),
                              ("dkT", [512, T], BF16), ("dqT", [D, 1024], BF16),
                              ("dao", [1024, D], F32), ("dx1", [1024, D], BF16),
                              ("dz", [1024, NFF], F32), ("dvk", [1024, 4], F32),
                              ("dhid", [1024, NFF], F32),
                              ("dcos", [128, T], BF16), ("dsin", [128, T], BF16),
                              ("dffn", [1024, D], F32), ("dqT2", [D, 512], BF16),
                              ("dprobe", [8, 128, 512], BF16),
                              ("dkT2", [512, T], BF16), ("dv2", [T, 512], BF16)]:
            dbg_t[name] = dram(name, shp, dt, kind="ExternalOutput")

    R = Rec(nc, ctx)
    sb = lambda n, shp, dt: E(nc.sbuf_tensor(n, shp, dt))

    p_a = E(nc.psum_tensor("pA", [128, 512], F32))
    p_b = E(nc.psum_tensor("pB", [128, 512], F32))
    p_c = E(nc.psum_tensor("pC", [128, 512], F32))
    p_t = E(nc.psum_tensor("pT", [128, 1024], BF16))
    p_mm = E(nc.psum_tensor("pMM", [128, 2048], F32))

    s_id = sb("c_id", [128, 128], BF16)
    s_ones = sb("c_ones", [1, 128], F32)
    s_onec = sb("c_onec", [128, 1], F32)
    s_iota = sb("c_iota", [128, 16], F32)
    s_cm = sb("c_cm", [128, 512], F32)
    s_eps = sb("c_eps", [128, 1], F32)
    s_fhi = sb("c_fhi", [1, 128], F32)
    s_flo = sb("c_flo", [1, 128], F32)
    s_bqc = sb("bqcs", [128, 16], F32)
    s_bkc = sb("bkcs", [128, 4], F32)
    s_bvb = sb("bvb", [128, 512], F32)
    s_row = sb("rowbuf", [1, 512], F32)
    s_tau_s = sb("taus", [128, 1], F32)
    s_tau_o = sb("tauo", [128, 1], F32)
    s_ntau_s = sb("ntaus", [128, 1], F32)
    s_ntau_o = sb("ntauo", [128, 1], F32)
    s_nss = sb("nss", [128, 1], F32)
    s_nso = sb("nso", [128, 1], F32)
    s_m3 = sb("m3", [128, NT_OWN, 2], F32)
    sm = {n: sb("sm_" + n, [128, 1], F32) for n in
          ("mu", "sq", "var", "rstd", "t0", "t1", "t2", "m", "negm", "rs", "pr",
           "lo", "hi", "mid", "chi", "cn", "vk", "r")}
    s_sqscr = sb("sqscr", [128, D], F32)
    s_part = sb("part", [128, 128], F32)
    s_pred = sb("predu8", [128, 1], mybir.dt.uint8)

    def dma(e, out, in_, reads=(), writes=()):
        R.op(e, lambda g: g.dma_start(out, in_), reads=reads, writes=writes, dma=True)

    def mm(out_ap, lhs_ap, rhs_ap, reads, pw, start, stop, mark=None):
        R.op("t", lambda t: t.matmul(out_ap, lhs_ap, rhs_ap, start=start, stop=stop),
             reads=list(reads), writes=[pw], mark=(stop if mark is None else mark))

    def tr(out_ap, in_ap, reads, mark):
        R.op("t", lambda t: t.transpose(out_ap, in_ap, s_id[:, :]),
             reads=list(reads) + ["c_id"], writes=["pT"], mark=mark)

    def v_op(fn, reads, writes):
        R.op("v", fn, reads=reads, writes=writes)

    def s_op(fn, reads, writes):
        R.op("s", fn, reads=reads, writes=writes)

    def ln_tile(xap, xres, outap, outres, n=D, save=None):
        v_op(lambda v: v.tensor_reduce(sm["mu"][:, :], xap, axis=AX.X, op=ALU.add),
             [xres], ["mu"])
        s_op(lambda s: s.activation(s_sqscr[:, :n], xap, AF.Square, accum_out=sm["sq"][:, :]),
             [xres], ["sqscr", "sq"])
        v_op(lambda v: v.tensor_scalar(sm["mu"][:, :], sm["mu"][:, :], 1.0 / n, None,
                                       op0=ALU.mult), ["mu"], ["mu"])
        v_op(lambda v: v.tensor_mul(sm["var"][:, :], sm["mu"][:, :], sm["mu"][:, :]),
             ["mu"], ["var"])
        v_op(lambda v: v.scalar_tensor_tensor(sm["var"][:, :], sm["sq"][:, :], 1.0 / n,
                                              sm["var"][:, :], op0=ALU.mult, op1=ALU.subtract),
             ["sq", "var"], ["var"])
        s_op(lambda s: s.activation(sm["rstd"][:, :], sm["var"][:, :], AF.Ln, bias=s_eps[:, :]),
             ["var", "c_eps"], ["rstd"])
        s_op(lambda s: s.activation(sm["rstd"][:, :], sm["rstd"][:, :], AF.Exp, scale=-0.5),
             ["rstd"], ["rstd"])
        if outap is not None:
            v_op(lambda v: v.tensor_scalar(outap, xap, sm["mu"][:, :], sm["rstd"][:, :],
                                           op0=ALU.subtract, op1=ALU.mult),
                 [xres, "mu", "rstd"], [outres])
        if save is not None:
            ap_, res_ = save
            v_op(lambda v: v.tensor_copy(ap_[:, 0:1], sm["mu"][:, :]), ["mu"], [res_])
            v_op(lambda v: v.tensor_copy(ap_[:, 1:2], sm["rstd"][:, :]), ["rstd"], [res_])

    def transpose16(src_ap_fn, srcres, dst_ap_fn, dstres, nch=DBLK):
        for c8 in range(0, nch, 8):
            for k in range(8):
                tr(p_t[:, k * 128:(k + 1) * 128], src_ap_fn(c8 + k), [srcres], mark=(k == 7))
            for k in range(8):
                v_op(lambda v, c8=c8, k=k: v.tensor_copy(dst_ap_fn(c8 + k),
                                                         p_t[:, k * 128:(k + 1) * 128]),
                     ["pT"], [dstres])

    def bcast_dram(dten, dst_ap, dst_res, n, off=0):
        for i in range(0, n, 512):
            w = min(512, n - i)
            dma("y", s_row[:1, :w], dten.ap()[:1, off + i:off + i + w],
                reads=["rowbuf"], writes=["rowbuf"])
            mm(p_c[:, :w], s_ones[:1, :], s_row[:1, :w],
               ["rowbuf", "c_ones"], "pC", True, True)
            v_op(lambda v, i=i, w=w: v.tensor_copy(dst_ap[:, i:i + w], p_c[:, :w]),
                 ["pC"], [dst_res])

    def scalar_stat(partial_slice, pres, nsdst, nsres, taudst, taures):
        v_op(lambda v: v.tensor_reduce(sm["t0"][:, :], partial_slice, axis=AX.X, op=ALU.add),
             [pres], ["t0"])
        mm(p_c[:1, 0:1], sm["t0"][:, :1], s_onec[:, :1], ["t0", "c_onec"], "pC", True, True)
        v_op(lambda v: v.tensor_scalar(sm["t1"][:1, :1], p_c[:1, 0:1], 1.0 / (NFF * D), None,
                                       op0=ALU.mult), ["pC"], ["t1"])
        v_op(lambda v: v.tensor_scalar(sm["t2"][:1, :1], sm["t1"][:1, :1], 1e-8, None,
                                       op0=ALU.add), ["t1"], ["t2"])
        mm(p_c[:, 0:1], s_ones[:1, :], sm["t1"][:1, :1], ["t1", "c_ones"], "pC", True, True)
        v_op(lambda v: v.tensor_scalar(nsdst[:, :], p_c[:, 0:1], -1.0, None, op0=ALU.mult),
             ["pC"], [nsres])
        mm(p_c[:, 1:2], s_ones[:1, :], sm["t2"][:1, :1], ["t2", "c_ones"], "pC", True, True)
        v_op(lambda v, taudst=taudst: v.tensor_scalar(taudst[:, :], p_c[:, 1:2], 0.5, None,
                                                      op0=ALU.mult), ["pC"], [taures])

    dma("y", s_id[:, :], d_id.ap(), writes=["c_id"])
    dma("y", s_ones[:1, :], d_ones.ap(), writes=["c_ones"])
    dma("y", s_iota[:, :], d_iota.ap(), writes=["c_iota"])
    dma("y", s_cm[:, :], d_cm.ap(), writes=["c_cm"])
    dma("y", s_fhi[:1, :], d_fhi.ap(), writes=["fhi"])
    dma("y", s_flo[:1, :], d_flo.ap(), writes=["flo"])
    R.op("v", lambda v: v.memset(s_eps[:, :], EPS), writes=["c_eps"])
    R.op("v", lambda v: v.memset(s_onec[:, :], 1.0), writes=["c_onec"])

    with contextlib.ExitStack() as pa:
        sba = lambda n, shp, dt: pa.enter_context(nc.sbuf_tensor(n, shp, dt))
        s_cosf = sba("cosf", [128, T], BF16)
        s_sinf = sba("sinf", [128, T], BF16)
        s_coso = sba("coso", [128, 1024], BF16)
        s_sino = sba("sino", [128, 1024], BF16)
        s_kT = sba("kT", [128, 4, T], BF16)
        s_v = sba("vv", [128, NTILE_SEQ, 512], BF16)
        s_wbig = sba("wbig", [128, D], F32)
        s_tst = sba("tst", [128, 2048], BF16)
        s_stg = sba("stg", [128, 8, 128], BF16)

        with contextlib.ExitStack() as p0:
            sb0 = lambda n, shp, dt: p0.enter_context(nc.sbuf_tensor(n, shp, dt))
            t_bq = sb0("t_bq", [128, 16], BF16)
            t_bk = sb0("t_bk", [128, 4], BF16)
            t_bv = sb0("t_bv", [1, 512], BF16)
            t_bvf = sb0("t_bvf", [1, 512], F32)
            dma("y", t_bq[:, :], d_bqc.ap(), writes=["tbq"])
            dma("y", t_bk[:, :], d_bkc.ap(), writes=["tbk"])
            dma("y", t_bv[:1, :], d_bvr.ap(), writes=["tbv"])
            v_op(lambda v: v.tensor_copy(s_bqc[:, :], t_bq[:, :]), ["tbq"], ["bqcs"])
            v_op(lambda v: v.tensor_copy(s_bkc[:, :], t_bk[:, :]), ["tbk"], ["bkcs"])
            v_op(lambda v: v.tensor_copy(t_bvf[:1, :], t_bv[:1, :]), ["tbv"], ["tbvf"])
            mm(p_c[:, :], s_ones[:1, :], t_bvf[:1, :], ["tbvf", "c_ones"], "pC", True, True)
            v_op(lambda v: v.tensor_copy(s_bvb[:, :], p_c[:, :]), ["pC"], ["bvb"])

            s_ang = sb0("w_ang", [128, T], F32)
            s_th = sb0("w_th", [128, T], F32)
            s_k1 = sb0("w_k1", [128, T], F32)
            s_k2 = sb0("w_k2", [128, T], F32)
            s_trw = sb0("w_trw", [1, T], F32)

            def trig(ks_d, kc_d, trow_d, n, cosb, sinb, cres, sres):
                dma("y", s_k1[:, :n], ks_d.ap(), reads=["k1"], writes=["k1"])
                dma("y", s_k2[:, :n], kc_d.ap(), reads=["k2"], writes=["k2"])
                dma("y", s_trw[:1, :n], trow_d.ap(), reads=["trow"], writes=["trow"])
                for i in range(n // 512):
                    mm(p_c[:, :], s_fhi[:1, :], s_trw[:1, i * 512:(i + 1) * 512],
                       ["fhi", "trow"], "pC", True, False)
                    mm(p_c[:, :], s_flo[:1, :], s_trw[:1, i * 512:(i + 1) * 512],
                       ["flo", "trow"], "pC", False, True)
                    v_op(lambda v, i=i: v.tensor_copy(s_ang[:, i * 512:(i + 1) * 512],
                                                      p_c[:, :]), ["pC"], ["ang"])
                first = [True]
                for c_ in (CW_C1, CW_C2, CW_C3):
                    sap = s_ang if first[0] else s_th
                    srs = "ang" if first[0] else "th"
                    v_op(lambda v, c_=c_, sap=sap: v.scalar_tensor_tensor(
                            s_th[:, :n], s_k1[:, :n], -c_, sap[:, :n],
                            op0=ALU.mult, op1=ALU.add), ["k1", srs], ["th"])
                    first[0] = False
                s_op(lambda s: s.activation(sinb[:, :n], s_th[:, :n], AF.Sin), ["th"], [sres])
                v_op(lambda v: v.tensor_scalar(s_th[:, :n], s_ang[:, :n], HPI, None,
                                               op0=ALU.add), ["ang"], ["th"])
                for c_ in (CW_C1, CW_C2, CW_C3):
                    v_op(lambda v, c_=c_: v.scalar_tensor_tensor(
                            s_th[:, :n], s_k2[:, :n], -c_, s_th[:, :n],
                            op0=ALU.mult, op1=ALU.add), ["k2", "th"], ["th"])
                s_op(lambda s: s.activation(cosb[:, :n], s_th[:, :n], AF.Sin), ["th"], [cres])

            trig(d_ksf, d_kcf, d_tf, T, s_cosf, s_sinf, "cosF", "sinF")
            trig(d_kso, d_kco, d_to, 1024, s_coso, s_sino, "cosO", "sinO")
            if dbg:
                dma("g", dbg_t["dcos"].ap(), s_cosf[:, :], reads=["cosF"])
                dma("g", dbg_t["dsin"].ap(), s_sinf[:, :], reads=["sinF"])
            R.barrier()

        def mk_wsyn_scale(i):
            def f1():
                dma("y", s_wbig[:, :], d_wsyn.ap()[i * 128:(i + 1) * 128, :],
                    reads=["wbig"], writes=["wbig"])

            def f2():
                s_op(lambda s: s.activation(s_sqscr[:, :], s_wbig[:, :], AF.Abs,
                                            accum_out=s_part[:, i:i + 1]),
                     ["wbig"], ["sqscr", "part"])
            return [f1, f2]

        def mk_wout_scale(i, h):
            def f1():
                dma("y", s_wbig[:, :],
                    d_wout.ap()[i * 128:(i + 1) * 128, h * 2048:(h + 1) * 2048],
                    reads=["wbig"], writes=["wbig"])

            def f2():
                v_op(lambda v: v.tensor_reduce(s_part[:, 64 + i * 4 + h:64 + i * 4 + h + 1],
                                               s_wbig[:, :], axis=AX.X, op=ALU.add,
                                               apply_absolute_value=True),
                     ["wbig"], ["part2"])
            return [f1, f2]

        def mk_tsyn(i):
            def f1():
                dma("g", s_wbig[:, :], d_wsyn.ap()[i * 128:(i + 1) * 128, :],
                    reads=["wbig"], writes=["wbig"])

            def f2():
                v_op(lambda v: v.tensor_scalar(s_tst[:, :D], s_wbig[:, :], s_tau_s[:, :],
                                               None, op0=ALU.is_gt),
                     ["wbig", "taus"], ["tst"])
                v_op(lambda v: v.scalar_tensor_tensor(s_tst[:, :D], s_wbig[:, :],
                                                      s_ntau_s[:, :], s_tst[:, :D],
                                                      op0=ALU.is_lt, op1=ALU.subtract),
                     ["wbig", "ntaus", "tst"], ["tst"])

            def f3():
                for c8 in range(0, DBLK, 8):
                    for k in range(8):
                        tr(p_t[:, k * 128:(k + 1) * 128],
                           s_tst[:, (c8 + k) * 128:(c8 + k + 1) * 128], ["tst"], mark=(k == 7))
                    for k in range(8):
                        v_op(lambda v, k=k: v.tensor_copy(s_stg[:, k, :],
                                                          p_t[:, k * 128:(k + 1) * 128]),
                             ["pT"], ["stg"])
                    dma("g", d_tsyn.ap()[c8:c8 + 8, :, i * 128:(i + 1) * 128]
                            .rearrange("c p u -> p c u"),
                        s_stg[:, :, :], reads=["stg"], writes=["TSYN"])
            return [f1, f2, f3]

        def mk_tout(i, q):
            def f1():
                dma("g", s_wbig[:, :],
                    d_wout.ap()[i * 128:(i + 1) * 128, q * 2048:(q + 1) * 2048],
                    reads=["wbig"], writes=["wbig"])

            def f2():
                v_op(lambda v: v.tensor_scalar(s_tst[:, :], s_wbig[:, :], s_tau_o[:, :],
                                               None, op0=ALU.is_gt),
                     ["wbig", "tauo"], ["tst"])
                v_op(lambda v: v.scalar_tensor_tensor(s_tst[:, :], s_wbig[:, :],
                                                      s_ntau_o[:, :], s_tst[:, :],
                                                      op0=ALU.is_lt, op1=ALU.subtract),
                     ["wbig", "ntauo", "tst"], ["tst"])

            def f3():
                for u8 in range(0, 16, 8):
                    for k in range(8):
                        tr(p_t[:, k * 128:(k + 1) * 128],
                           s_tst[:, (u8 + k) * 128:(u8 + k + 1) * 128], ["tst"], mark=(k == 7))
                    for k in range(8):
                        v_op(lambda v, k=k: v.tensor_copy(s_stg[:, k, :],
                                                          p_t[:, k * 128:(k + 1) * 128]),
                             ["pT"], ["stg"])
                    dma("g", d_tout.ap()[q * 16 + u8:q * 16 + u8 + 8, :,
                                         i * 128:(i + 1) * 128]
                            .rearrange("u p m -> p u m"),
                        s_stg[:, :, :], reads=["stg"], writes=["TOUT"])
            return [f1, f2, f3]

        bgq = []
        bqi = [0]

        def pump(n):
            for _ in range(n):
                if bqi[0] < len(bgq):
                    bgq[bqi[0]]()
                    bqi[0] += 1

        def pump_all():
            pump(len(bgq) - bqi[0])

        def set_bg(tasks):
            pump_all()
            bgq.clear()
            bqi[0] = 0
            bgq.extend(tasks)

        tasks = []
        for i in range(UCH):
            tasks += mk_wsyn_scale(i)
        set_bg(tasks)
        with contextlib.ExitStack() as p2:
            sb2 = lambda n, shp, dt: p2.enter_context(nc.sbuf_tensor(n, shp, dt))
            s_xin = [sb2(f"xin{i}", [128, D], BF16) for i in range(2)]
            s_xln = [sb2(f"xln{i}", [128, D], BF16) for i in range(2)]
            s_xlT = sb2("xlT", [128, DBLK, GRP], BF16)
            s_wk = sb2("wks", [128, DBLK, 512], BF16)
            s_wv = sb2("wvs", [128, DBLK, 512], BF16)
            s_rt1 = sb2("rt1", [128, GRP], BF16)
            s_rt2 = sb2("rt2", [128, GRP], BF16)
            s_rt3 = sb2("rt3", [128, GRP], BF16)

            dma("y", s_wk[:, :, :], d_wk.ap().rearrange("(c p) f -> p c f", p=128),
                writes=["wks"])
            dma("y", s_wv[:, :, :], d_wv.ap().rearrange("(c p) f -> p c f", p=128),
                writes=["wvs"])

            def rope(dst_ap, dst_res, src_ap, src_res, c0):
                v_op(lambda v: v.tensor_scalar(s_rt2[0:64, :], src_ap[64:128, :], -1.0, None,
                                               op0=ALU.mult), [src_res], ["rt2"])
                v_op(lambda v: v.tensor_copy(s_rt2[64:128, :], src_ap[0:64, :]),
                     [src_res], ["rt2"])
                v_op(lambda v: v.tensor_mul(s_rt3[:, :], src_ap, s_cosf[:, c0:c0 + GRP]),
                     [src_res, "cosF"], ["rt3"])
                v_op(lambda v: v.tensor_mul(s_rt2[:, :], s_rt2[:, :], s_sinf[:, c0:c0 + GRP]),
                     ["rt2", "sinF"], ["rt2"])
                v_op(lambda v: v.tensor_add(dst_ap, s_rt3[:, :], s_rt2[:, :]),
                     ["rt3", "rt2"], [dst_res])

            for g in range(T // GRP):
                for tt_ in range(2):
                    it = g * 2 + tt_
                    pb = it % 2
                    dma("y", s_xin[pb][:, :], d_xf.ap()[it * 128:(it + 1) * 128, :],
                        reads=[f"xin{pb}"], writes=[f"xin{pb}"])
                    ln_tile(s_xin[pb][:, :], f"xin{pb}", s_xln[pb][:, :], f"xln{pb}")
                    if dbg:
                        dma("g", dbg_t["dxln"].ap()[it * 128:(it + 1) * 128, :],
                            s_xln[pb][:, :], reads=[f"xln{pb}"])
                    transpose16(lambda c, pb=pb: s_xln[pb][:, c * 128:(c + 1) * 128],
                                f"xln{pb}",
                                lambda c, tt_=tt_: s_xlT[:, c, tt_ * 128:(tt_ + 1) * 128],
                                "xlT")
                    for c in range(DBLK):
                        mm(p_b[:, :], s_xlT[:, c, tt_ * 128:(tt_ + 1) * 128], s_wv[:, c, :],
                           ["xlT", "wvs"], "pB", c == 0, c == DBLK - 1)
                    v_op(lambda v, it=it: v.tensor_add(s_v[:, it, :], p_b[:, :], s_bvb[:, :]),
                         ["pB", "bvb"], [f"vt{it}"])
                    if dbg:
                        dma("g", dbg_t["dv"].ap()[it * 128:(it + 1) * 128, :],
                            s_v[:, it, :], reads=[f"vt{it}"])
                    pump(4)
                for f in range(4):
                    pp, ppr = (p_a, "pA") if f % 2 == 0 else (p_b, "pB")
                    for c in range(DBLK):
                        mm(pp[:, :GRP], s_wk[:, c, f * 128:(f + 1) * 128], s_xlT[:, c, :],
                           ["wks", "xlT"], ppr, c == 0, c == DBLK - 1)
                    s_op(lambda s, pp=pp, f=f: s.activation(s_rt1[:, :], pp[:, :GRP],
                                                            AF.Identity,
                                                            bias=s_bkc[:, f:f + 1]),
                         [ppr, "bkcs"], ["rt1"])
                    rope(s_kT[:, f, g * GRP:(g + 1) * GRP], f"kT{f}", s_rt1[:, :], "rt1",
                         g * GRP)
                pump(2)
            pump_all()
            if dbg:
                for f in range(4):
                    dma("g", dbg_t["dkT"].ap()[f * 128:(f + 1) * 128, :], s_kT[:, f, :],
                        reads=[f"kT{f}"])
            scalar_stat(s_part[:, 0:64], "part", s_nss, "nss", s_tau_s, "taus")
            v_op(lambda v: v.tensor_scalar(s_ntau_s[:, :], s_tau_s[:, :], -1.0, None,
                                           op0=ALU.mult), ["taus"], ["ntaus"])
            R.barrier()

        with contextlib.ExitStack() as p3:
            sb3 = lambda n, shp, dt: p3.enter_context(nc.sbuf_tensor(n, shp, dt))
            s_sc = sb3("scrow", [128, T], BF16)
            s_pp = [sb3(f"prow{i}", [128, T], BF16) for i in range(2)]
            s_ring = [sb3(f"ring{i}", [128, 256], BF16) for i in range(2)]
            s_avt = sb3("avt", [128, 16, 512], BF16)
            s_ao = sb3("ao", [128, D], F32)
            s_xot = sb3("xot", [128, D], BF16)
            s_qrt1 = sb3("qrt1", [128, 512], BF16)
            s_qrt2 = sb3("qrt2", [128, 512], BF16)
            s_qrt3 = sb3("qrt3", [128, 512], BF16)

            def rope_q(dst_ap, dst_res, src_ap, src_res, batch):
                c0 = batch * 512
                v_op(lambda v: v.tensor_scalar(s_qrt2[0:64, :], src_ap[64:128, :], -1.0,
                                               None, op0=ALU.mult), [src_res], ["qrt2"])
                v_op(lambda v: v.tensor_copy(s_qrt2[64:128, :], src_ap[0:64, :]),
                     [src_res], ["qrt2"])
                v_op(lambda v: v.tensor_mul(s_qrt3[:, :], src_ap, s_coso[:, c0:c0 + 512]),
                     [src_res, "cosO"], ["qrt3"])
                v_op(lambda v: v.tensor_mul(s_qrt2[:, :], s_qrt2[:, :],
                                            s_sino[:, c0:c0 + 512]),
                     ["qrt2", "sinO"], ["qrt2"])
                v_op(lambda v: v.tensor_add(dst_ap, s_qrt3[:, :], s_qrt2[:, :]),
                     ["qrt3", "qrt2"], [dst_res])

            tasks = []
            for i in range(DBLK):
                for h in range(4):
                    tasks += mk_wout_scale(i, h)
            for i in range(UCH):
                tasks += mk_tsyn(i)
            set_bg(tasks)

            for batch in range(2):
                with contextlib.ExitStack() as pq:
                    s_qT = pq.enter_context(nc.sbuf_tensor(f"qT{batch}", [128, 16, 512],
                                                           BF16))
                    s_wqb = [pq.enter_context(nc.sbuf_tensor(f"wqb{batch}_0",
                                                             [128, DBLK, 128], BF16))]
                    s_xoT = pq.enter_context(nc.sbuf_tensor(f"xoT{batch}",
                                                            [128, DBLK, 512], BF16))
                    for jj in range(4):
                        j = batch * 4 + jj
                        dma("y", s_xot[:, :], d_xo.ap()[j * 128:(j + 1) * 128, :],
                            reads=["xot"], writes=["xot"])
                        ln_tile(s_xot[:, :], "xot", s_xot[:, :], "xot")
                        transpose16(lambda c: s_xot[:, c * 128:(c + 1) * 128], "xot",
                                    lambda c, jj=jj: s_xoT[:, c, jj * 128:(jj + 1) * 128],
                                    "xoT")
                        pump(3)
                    for f in range(16):
                        pbq = 0
                        dma("y", s_wqb[pbq][:, :, :],
                            d_wq.ap().rearrange("(c p) f -> p c f",
                                                p=128)[:, :, f * 128:(f + 1) * 128],
                            reads=[f"wqb{pbq}"], writes=[f"wqb{pbq}"])
                        pp, ppr = (p_a, "pA") if f % 2 == 0 else (p_b, "pB")
                        for c in range(DBLK):
                            mm(pp[:, :], s_wqb[pbq][:, c, :], s_xoT[:, c, :],
                               [f"wqb{pbq}", "xoT"], ppr, c == 0, c == DBLK - 1)
                        s_op(lambda s, pp=pp, f=f: s.activation(s_qrt1[:, :], pp[:, :],
                                                                AF.Identity,
                                                                bias=s_bqc[:, f:f + 1]),
                             [ppr, "bqcs"], ["qrt1"])
                        rope_q(s_qT[:, f, :], f"qT{f}", s_qrt1[:, :], "qrt1", batch)
                        pump(2)
                    if dbg:
                        for f in range(16):
                            dma("g", dbg_t["dqT"].ap()[f * 128:(f + 1) * 128,
                                                       batch * 512:(batch + 1) * 512],
                                s_qT[:, f, :], reads=[f"qT{f}"])
                    for h in range(16):
                        g_kv = h // 4
                        for sub in range(2):
                            for jj2 in range(2):
                                jj = sub * 2 + jj2
                                j = batch * 4 + jj
                                nkb = j + 1
                                R.op("v", lambda v: v.memset(sm["m"][:, :], -3e38),
                                     writes=["m"])
                                for kb in range(nkb):
                                    pp, ppr = (p_a, "pA") if kb % 2 == 0 else (p_b, "pB")
                                    mm(pp[:, :], s_qT[:, h, jj * 128:(jj + 1) * 128],
                                       s_kT[:, g_kv, kb * 512:(kb + 1) * 512],
                                       [f"qT{h}", f"kT{g_kv}"], ppr, True, True)
                                    if kb == nkb - 1:
                                        v_op(lambda v, pp=pp: v.tensor_add(
                                                pp[:, :], pp[:, :], s_cm[:, :]),
                                             [ppr, "c_cm"], [ppr])
                                    v_op(lambda v, pp=pp, kb=kb: v.tensor_scalar(
                                            s_sc[:, kb * 512:(kb + 1) * 512], pp[:, :],
                                            SCALE, None, op0=ALU.mult, op1=ALU.max,
                                            accum_out=sm["t0"][:, :]),
                                         [ppr], ["scrow", "t0"])
                                    v_op(lambda v: v.tensor_max(sm["m"][:, :], sm["m"][:, :],
                                                                sm["t0"][:, :]),
                                         ["m", "t0"], ["m"])
                                v_op(lambda v: v.tensor_scalar(sm["negm"][:, :],
                                                               sm["m"][:, :], -1.0, None,
                                                               op0=ALU.mult),
                                     ["m"], ["negm"])
                                s_op(lambda s, jj2=jj2, nkb=nkb: s.activation(
                                        s_pp[jj2][:, :nkb * 512], s_sc[:, :nkb * 512],
                                        AF.Exp, bias=sm["negm"][:, :],
                                        accum_out=sm["rs"][:, :]),
                                     ["scrow", "negm"], [f"prow{jj2}", "rs"])
                                v_op(lambda v: v.reciprocal(sm["pr"][:, :], sm["rs"][:, :]),
                                     ["rs"], ["pr"])
                                v_op(lambda v, jj2=jj2, nkb=nkb: v.tensor_scalar(
                                        s_pp[jj2][:, :nkb * 512], s_pp[jj2][:, :nkb * 512],
                                        sm["pr"][:, :], None, op0=ALU.mult),
                                     [f"prow{jj2}", "pr"], [f"prow{jj2}"])
                            j1 = batch * 4 + sub * 2 + 1
                            nkb128 = 4 * (j1 + 1)
                            for kb in range(nkb128):
                                rpb = kb % 2
                                for jj2 in range(2):
                                    j = batch * 4 + sub * 2 + jj2
                                    if kb < 4 * (j + 1):
                                        tr(p_t[:, jj2 * 128:(jj2 + 1) * 128],
                                           s_pp[jj2][:, kb * 128:(kb + 1) * 128],
                                           [f"prow{jj2}"], mark=True)
                                        v_op(lambda v, rpb=rpb, jj2=jj2: v.tensor_copy(
                                                s_ring[rpb][:, jj2 * 128:(jj2 + 1) * 128],
                                                p_t[:, jj2 * 128:(jj2 + 1) * 128]),
                                             ["pT"], [f"ring{rpb}"])
                                    else:
                                        R.op("v", lambda v, rpb=rpb, jj2=jj2: v.memset(
                                                s_ring[rpb][:, jj2 * 128:(jj2 + 1) * 128],
                                                0.0),
                                             writes=[f"ring{rpb}"])
                                mm(p_c[:, sub * 256:(sub + 1) * 256],
                                   s_v[:, kb, g_kv * 128:(g_kv + 1) * 128],
                                   s_ring[rpb][:, :],
                                   [f"vt{kb}", f"ring{rpb}"], "pC", kb == 0,
                                   kb == nkb128 - 1)
                            s_op(lambda s, sub=sub, h=h: s.activation(
                                    s_avt[:, h, sub * 256:(sub + 1) * 256],
                                    p_c[:, sub * 256:(sub + 1) * 256], AF.Identity),
                                 ["pC"], [f"avt{h}"])
                            if dbg and batch == 1 and h < 2:
                                idx = h * 2 + sub
                                dma("g", dbg_t["dprobe"].ap()[idx, :, :],
                                    s_qT[:, 0, :], reads=[f"qT0", f"avt{h}"])
                        pump(4)
                    if dbg and batch == 1:
                        for f in range(16):
                            dma("g", dbg_t["dqT2"].ap()[f * 128:(f + 1) * 128, :],
                                s_qT[:, f, :], reads=[f"qT{f}"])
                        for f in range(4):
                            dma("g", dbg_t["dkT2"].ap()[f * 128:(f + 1) * 128, :],
                                s_kT[:, f, :], reads=[f"kT{f}"])
                        for it_ in range(NTILE_SEQ):
                            dma("g", dbg_t["dv2"].ap()[it_ * 128:(it_ + 1) * 128, :],
                                s_v[:, it_, :], reads=[f"vt{it_}"])
                R.barrier()
                with contextlib.ExitStack() as po:
                    s_wob = [po.enter_context(nc.sbuf_tensor(f"wob{batch}_{i}", [128, D],
                                                             BF16)) for i in range(2)]
                    for jj in range(4):
                        j = batch * 4 + jj
                        for hh in range(16):
                            pbo = hh % 2
                            dma("y", s_wob[pbo][:, :], d_wo.ap()[hh * 128:(hh + 1) * 128, :],
                                reads=[f"wob{pbo}"], writes=[f"wob{pbo}"])
                            for db in range(4):
                                mm(p_mm[:, db * 512:(db + 1) * 512],
                                   s_avt[:, hh, jj * 128:(jj + 1) * 128],
                                   s_wob[pbo][:, db * 512:(db + 1) * 512],
                                   [f"avt{hh}", f"wob{pbo}"], "pMM", hh == 0, hh == 15,
                                   mark=(db == 3))
                        v_op(lambda v: v.tensor_copy(s_ao[:, :], p_mm[:, :]),
                             ["pMM"], ["ao"])
                        if dbg:
                            dma("g", dbg_t["dao"].ap()[j * 128:(j + 1) * 128, :],
                                s_ao[:, :], reads=["ao"])
                        ln_tile(s_ao[:, :], "ao", s_sqscr[:, :], "sqscr")
                        dma("y", s_xot[:, :], d_xo.ap()[j * 128:(j + 1) * 128, :],
                            reads=["xot"], writes=["xot"])
                        v_op(lambda v: v.tensor_add(s_xot[:, :], s_sqscr[:, :],
                                                    s_xot[:, :]),
                             ["sqscr", "xot"], ["xot"])
                        dma("g", d_x1.ap()[j * 128:(j + 1) * 128, :], s_xot[:, :],
                            reads=["xot"], writes=[f"x1_{j}"])
                        if dbg:
                            dma("g", dbg_t["dx1"].ap()[j * 128:(j + 1) * 128, :],
                                s_xot[:, :], reads=["xot"])
                        pump(6)
                R.barrier()
                if batch == 0:
                    pump_all()
                    scalar_stat(s_part[:, 64:128], "part2", s_nso, "nso", s_tau_o, "tauo")
                    v_op(lambda v: v.tensor_scalar(s_ntau_o[:, :], s_tau_o[:, :], -1.0,
                                                   None, op0=ALU.mult), ["tauo"], ["ntauo"])
                    tasks = []
                    for i in range(DBLK):
                        for q_ in range(4):
                            tasks += mk_tout(i, q_)
                    set_bg(tasks)
            pump_all()
            R.barrier()

    with contextlib.ExitStack() as p4:
        sb4 = lambda n, shp, dt: p4.enter_context(nc.sbuf_tensor(n, shp, dt))
        s_z = [sb4(f"z{i}", [128, NFF], F32) for i in range(2)]
        s_u = sb4("ubuf", [128, 4096], F32)
        s_A = sb4("Abuf", [128, NFF], BF16)
        s_Bb = sb4("Bbuf", [128, NFF], BF16)
        s_x2T = sb4("x2T", [128, DBLK, 256], BF16)
        s_x2 = sb4("x2", [128, D], BF16)
        s_x1t = sb4("x1t", [128, D], BF16)
        s_tsn = sb4("tsn", [128, DBLK, 512], BF16)
        s_ton = [sb4(f"ton{i}", [128, D], BF16) for i in range(2)]
        s_hiT = [sb4(f"hiT{i}", [128, 128], BF16) for i in range(2)]
        s_loT = [sb4(f"loT{i}", [128, 128], BF16) for i in range(2)]
        s_out = sb4("outb", [128, D], F32)
        s_bsb = sb4("bsb", [128, 512], F32)
        s_gb = sb4("gbb", [128, D], F32)
        s_c32 = sb4("c32", [128, 32], F32)
        s_m16 = sb4("m16", [128, 16], F32)
        s_m8 = sb4("m8", [128, 8], F32)
        s_cand = sb4("cand", [128, 16], F32)

        def topk_tile(zres, zap):
            v_op(lambda v: v.tensor_scalar(s_A[:, :], zap, 0.0, None, op0=ALU.add),
                 [zres], ["Abuf"])
            R.op("v", lambda v: v.memset(sm["lo"][:, :], -64.0), writes=["lo"])
            R.op("v", lambda v: v.memset(sm["hi"][:, :], 64.0), writes=["hi"])
            R.op("v", lambda v: v.memset(sm["chi"][:, :], 0.0), writes=["chi"])

            def mid_update():
                v_op(lambda v: v.tensor_add(sm["mid"][:, :], sm["lo"][:, :], sm["hi"][:, :]),
                     ["lo", "hi"], ["mid"])
                v_op(lambda v: v.tensor_scalar(sm["mid"][:, :], sm["mid"][:, :], 0.5, None,
                                               op0=ALU.mult), ["mid"], ["mid"])

            def branch(track):
                v_op(lambda v: v.tensor_scalar(s_pred[:, :], sm["cn"][:, :], float(KTOP),
                                               None, op0=ALU.is_ge), ["cn"], ["predu"])
                v_op(lambda v: v.select(sm["lo"][:, :], s_pred[:, :], sm["mid"][:, :],
                                        sm["lo"][:, :]), ["predu", "mid", "lo"], ["lo"])
                if track:
                    v_op(lambda v: v.select(sm["chi"][:, :], s_pred[:, :], sm["chi"][:, :],
                                            sm["cn"][:, :]), ["predu", "chi", "cn"], ["chi"])
                v_op(lambda v: v.select(sm["hi"][:, :], s_pred[:, :], sm["hi"][:, :],
                                        sm["mid"][:, :]), ["predu", "hi", "mid"], ["hi"])

            for _ in range(11):
                mid_update()
                v_op(lambda v: v.tensor_scalar(s_Bb[:, :], s_A[:, :], sm["mid"][:, :], None,
                                               op0=ALU.is_ge, op1=ALU.add,
                                               accum_out=sm["cn"][:, :]),
                     ["Abuf", "mid"], ["Bbuf", "cn"])
                branch(False)
            v_op(lambda v: v.tensor_scalar(sm["t0"][:, :], sm["lo"][:, :], -1.0, None,
                                           op0=ALU.mult), ["lo"], ["t0"])
            v_op(lambda v: v.tensor_max(sm["t0"][:, :], sm["t0"][:, :], sm["lo"][:, :]),
                 ["t0", "lo"], ["t0"])
            v_op(lambda v: v.scalar_tensor_tensor(sm["lo"][:, :], sm["t0"][:, :], -2.0**-7,
                                                  sm["lo"][:, :], op0=ALU.mult, op1=ALU.add),
                 ["t0", "lo"], ["lo"])
            v_op(lambda v: v.tensor_scalar(sm["lo"][:, :], sm["lo"][:, :], 1e-30, None,
                                           op0=ALU.subtract), ["lo"], ["lo"])
            v_op(lambda v: v.tensor_scalar(sm["t1"][:, :], sm["hi"][:, :], -1.0, None,
                                           op0=ALU.mult), ["hi"], ["t1"])
            v_op(lambda v: v.tensor_max(sm["t1"][:, :], sm["t1"][:, :], sm["hi"][:, :]),
                 ["t1", "hi"], ["t1"])
            v_op(lambda v: v.scalar_tensor_tensor(sm["hi"][:, :], sm["t1"][:, :], 2.0**-7,
                                                  sm["hi"][:, :], op0=ALU.mult, op1=ALU.add),
                 ["t1", "hi"], ["hi"])
            v_op(lambda v: v.tensor_scalar(sm["hi"][:, :], sm["hi"][:, :], 1e-30, None,
                                           op0=ALU.add), ["hi"], ["hi"])
            v_op(lambda v: v.tensor_scalar(s_Bb[:, :], zap, sm["hi"][:, :], None,
                                           op0=ALU.is_ge, op1=ALU.add,
                                           accum_out=sm["chi"][:, :]),
                 [zres, "hi"], ["Bbuf", "chi"])
            for _ in range(8):
                mid_update()
                v_op(lambda v: v.tensor_scalar(s_Bb[:, :], zap, sm["mid"][:, :], None,
                                               op0=ALU.is_ge, op1=ALU.add,
                                               accum_out=sm["cn"][:, :]),
                     [zres, "mid"], ["Bbuf", "cn"])
                branch(True)
            for half in range(2):
                zh = zap[:, half * 4096:(half + 1) * 4096]
                v_op(lambda v, zh=zh: v.scalar_tensor_tensor(s_u[:, :], zh, sm["hi"][:, :],
                                                             zh, op0=ALU.is_lt,
                                                             op1=ALU.mult),
                     [zres, "hi"], ["ubuf"])
                R.op("v", lambda v: v.max(out=s_m8[:, :], in_=s_u[:, :]),
                     reads=["ubuf"], writes=["m8"])
                v_op(lambda v, half=half: v.tensor_copy(s_c32[:, half * 16:half * 16 + 8],
                                                        s_m8[:, :]), ["m8"], ["c32"])
                R.op("v", lambda v: v.match_replace(out=s_u[:, :], in_to_replace=s_m8[:, :],
                                                    in_values=s_u[:, :], imm_value=NEG),
                     reads=["ubuf", "m8"], writes=["ubuf"])
                R.op("v", lambda v: v.max(out=s_m8[:, :], in_=s_u[:, :]),
                     reads=["ubuf"], writes=["m8"])
                v_op(lambda v, half=half: v.tensor_copy(
                        s_c32[:, half * 16 + 8:half * 16 + 16], s_m8[:, :]),
                     ["m8"], ["c32"])
            R.op("v", lambda v: v.max(out=s_m16[:, :8], in_=s_c32[:, :]),
                 reads=["c32"], writes=["m16"])
            R.op("v", lambda v: v.match_replace(out=s_c32[:, :], in_to_replace=s_m16[:, :8],
                                                in_values=s_c32[:, :], imm_value=NEG),
                 reads=["c32", "m16"], writes=["c32"])
            R.op("v", lambda v: v.max(out=s_m16[:, 8:], in_=s_c32[:, :]),
                 reads=["c32", "m16"], writes=["m16"])
            v_op(lambda v: v.tensor_scalar(sm["r"][:, :], sm["chi"][:, :], -1.0, float(KTOP),
                                           op0=ALU.mult, op1=ALU.add), ["chi"], ["r"])
            v_op(lambda v: v.scalar_tensor_tensor(s_cand[:, :], s_iota[:, :], sm["r"][:, :],
                                                  s_m16[:, :], op0=ALU.is_equal,
                                                  op1=ALU.mult, accum_out=sm["vk"][:, :]),
                 ["c_iota", "r", "m16"], ["cand", "vk"])

        for blk in range(4):
            tiles = (2 * blk, 2 * blk + 1)
            for tt, j in enumerate(tiles):
                dma("y", s_x1t[:, :], d_x1.ap()[j * 128:(j + 1) * 128, :],
                    reads=["x1t", f"x1_{j}"], writes=["x1t"])
                ln_tile(s_x1t[:, :], "x1t", s_x2[:, :], "x2", D,
                        save=(s_m3[:, j, :], f"m3_{j}"))
                transpose16(lambda c: s_x2[:, c * 128:(c + 1) * 128], "x2",
                            lambda c, tt=tt: s_x2T[:, c, tt * 128:(tt + 1) * 128], "x2T")
            for u in range(16):
                dma("y", s_tsn[:, :, :],
                    d_tsyn.ap()[:, :, u * 512:(u + 1) * 512].rearrange("c p u -> p c u"),
                    reads=["tsn", "TSYN"], writes=["tsn"])
                dma("y", s_row[:1, :], d_bsyn.ap()[:1, u * 512:(u + 1) * 512],
                    reads=["rowbuf"], writes=["rowbuf"])
                mm(p_c[:, :], s_ones[:1, :], s_row[:1, :],
                   ["rowbuf", "c_ones"], "pC", True, True)
                v_op(lambda v: v.tensor_copy(s_bsb[:, :], p_c[:, :]), ["pC"], ["bsb"])
                for tt in range(2):
                    pp, ppr = (p_a, "pA") if tt == 0 else (p_b, "pB")
                    for c in range(DBLK):
                        mm(pp[:, :], s_x2T[:, c, tt * 128:(tt + 1) * 128], s_tsn[:, c, :],
                           ["x2T", "tsn"], ppr, c == 0, c == DBLK - 1)
                    v_op(lambda v, pp=pp, tt=tt, u=u: v.scalar_tensor_tensor(
                            s_z[tt][:, u * 512:(u + 1) * 512], pp[:, :], s_nss[:, :],
                            s_bsb[:, :], op0=ALU.mult, op1=ALU.add),
                         [ppr, "nss", "bsb"], [f"z{tt}"])
            if dbg:
                for tt, j in enumerate(tiles):
                    dma("g", dbg_t["dz"].ap()[j * 128:(j + 1) * 128, :], s_z[tt][:, :],
                        reads=[f"z{tt}"])
            for tt, j in enumerate(tiles):
                zap = s_z[tt][:, :]
                zres = f"z{tt}"
                topk_tile(zres, zap)
                if dbg:
                    v_op(lambda v: v.tensor_copy(s_c32[:, 0:1], sm["vk"][:, :]),
                         ["vk"], ["c32"])
                    v_op(lambda v: v.tensor_copy(s_c32[:, 1:2], sm["chi"][:, :]),
                         ["chi"], ["c32"])
                    v_op(lambda v: v.tensor_copy(s_c32[:, 2:3], sm["hi"][:, :]),
                         ["hi"], ["c32"])
                    v_op(lambda v: v.tensor_copy(s_c32[:, 3:4], sm["r"][:, :]),
                         ["r"], ["c32"])
                    dma("g", dbg_t["dvk"].ap()[j * 128:(j + 1) * 128, :], s_c32[:, 0:4],
                        reads=["c32"])
                v_op(lambda v, zap=zap: v.scalar_tensor_tensor(zap, zap, sm["vk"][:, :],
                                                                zap, op0=ALU.is_ge,
                                                                op1=ALU.mult),
                     [zres, "vk"], [zres])
                if dbg:
                    dma("g", dbg_t["dhid"].ap()[j * 128:(j + 1) * 128, :], zap,
                        reads=[zres])
                v_op(lambda v, zap=zap: v.tensor_scalar(s_A[:, :], zap, 0.0, None,
                                                        op0=ALU.max),
                     [zres], ["Abuf"])
                v_op(lambda v, zap=zap: v.scalar_tensor_tensor(s_Bb[:, :], zap, 0.0,
                                                               s_A[:, :], op0=ALU.max,
                                                               op1=ALU.subtract),
                     [zres, "Abuf"], ["Bbuf"])
                for u in range(UCH):
                    pbT = u % 2
                    dma("y", s_ton[pbT][:, :], d_tout.ap()[u, :, :],
                        reads=[f"ton{pbT}", "TOUT"], writes=[f"ton{pbT}"])
                    tr(p_t[:, 0:128], s_A[:, u * 128:(u + 1) * 128], ["Abuf"], mark=False)
                    tr(p_t[:, 128:256], s_Bb[:, u * 128:(u + 1) * 128], ["Bbuf"], mark=True)
                    v_op(lambda v, pbT=pbT: v.tensor_copy(s_hiT[pbT][:, :], p_t[:, 0:128]),
                         ["pT"], [f"hiT{pbT}"])
                    v_op(lambda v, pbT=pbT: v.tensor_copy(s_loT[pbT][:, :],
                                                          p_t[:, 128:256]),
                         ["pT"], [f"loT{pbT}"])
                    for db in range(4):
                        mm(p_mm[:, db * 512:(db + 1) * 512], s_hiT[pbT][:, :],
                           s_ton[pbT][:, db * 512:(db + 1) * 512],
                           [f"hiT{pbT}", f"ton{pbT}"], "pMM", u == 0, False, mark=False)
                        mm(p_mm[:, db * 512:(db + 1) * 512], s_loT[pbT][:, :],
                           s_ton[pbT][:, db * 512:(db + 1) * 512],
                           [f"loT{pbT}", f"ton{pbT}"], "pMM", False, u == UCH - 1,
                           mark=(db == 3))
                bcast_dram(d_bout, s_gb, "gbb", D)
                v_op(lambda v: v.tensor_scalar(s_out[:, :], p_mm[:, :], s_nso[:, :], None,
                                               op0=ALU.mult), ["pMM", "nso"], ["outb"])
                v_op(lambda v: v.tensor_add(s_out[:, :], s_out[:, :], s_gb[:, :]),
                     ["outb", "gbb"], ["outb"])
                if dbg:
                    dma("g", dbg_t["dffn"].ap()[j * 128:(j + 1) * 128, :], s_out[:, :],
                        reads=["outb"])
                dma("y", s_x1t[:, :], d_x1.ap()[j * 128:(j + 1) * 128, :],
                    reads=["x1t", f"x1_{j}"], writes=["x1t"])
                v_op(lambda v, j=j: v.tensor_scalar(s_x2[:, :], s_x1t[:, :],
                                                    s_m3[:, j, 0:1], s_m3[:, j, 1:2],
                                                    op0=ALU.subtract, op1=ALU.mult),
                     ["x1t", f"m3_{j}"], ["x2"])
                v_op(lambda v: v.tensor_add(s_out[:, :], s_out[:, :], s_x2[:, :]),
                     ["outb", "x2"], ["outb"])
                ln_tile(s_out[:, :], "outb", s_sqscr[:, :], "sqscr")
                bcast_dram(d_gam, s_gb, "gbb", D)
                v_op(lambda v: v.tensor_mul(s_out[:, :], s_sqscr[:, :], s_gb[:, :]),
                     ["sqscr", "gbb"], ["outb"])
                bcast_dram(d_bet, s_gb, "gbb", D)
                v_op(lambda v: v.tensor_add(s_out[:, :], s_out[:, :], s_gb[:, :]),
                     ["outb", "gbb"], ["outb"])
                ln_tile(s_out[:, :], "outb", s_sqscr[:, :], "sqscr")
                v_op(lambda v: v.tensor_add(s_out[:, :], s_sqscr[:, :], s_x1t[:, :]),
                     ["sqscr", "x1t"], ["outb"])
                dma("g", d_y.ap()[j * 128:(j + 1) * 128, :], s_out[:, :],
                    reads=["outb"], writes=[f"y{j}"])
        R.barrier()

    with nc.Block() as block:
        R.finalize(block)
    return nc, dbg_t


def kernel(L0_attn_x, L0_attn_Wq, L0_attn_bq, L0_attn_Wk, L0_attn_bk,
           L0_attn_Wv, L0_attn_bv, L0_attn_Wo, L0_ffn_W_syn, L0_ffn_b_syn,
           L0_ffn_W_out, L0_ffn_b_out, L0_ffn_gamma, L0_ffn_beta, _dbg=False):
    x = np.asarray(L0_attn_x)
    in_maps = []
    with contextlib.ExitStack() as ctx:
        nc, dbg_t = build_program(ctx, dbg=_dbg)
        for core in range(8):
            b, r = core // 4, core % 4
            tb = host_tables(r)
            xb = np.ascontiguousarray(x[b])
            in_maps.append({
                "xf": xb,
                "xo": np.ascontiguousarray(xb[tb["own_rows"]]),
                "wq": np.asarray(L0_attn_Wq), "wk": np.asarray(L0_attn_Wk),
                "wv": np.asarray(L0_attn_Wv), "wo": np.asarray(L0_attn_Wo),
                "bqc": np.ascontiguousarray(np.asarray(L0_attn_bq).reshape(16, 128).T),
                "bkc": np.ascontiguousarray(np.asarray(L0_attn_bk).reshape(4, 128).T),
                "bvr": np.asarray(L0_attn_bv).reshape(1, 512),
                "wsyn": np.asarray(L0_ffn_W_syn), "wout": np.asarray(L0_ffn_W_out),
                "bsyn": np.asarray(L0_ffn_b_syn).reshape(1, NFF),
                "bout": np.asarray(L0_ffn_b_out).reshape(1, D),
                "gam": np.asarray(L0_ffn_gamma).reshape(1, D),
                "bet": np.asarray(L0_ffn_beta).reshape(1, D),
                "cfhi": tb["fhi"], "cflo": tb["flo"], "ctf": tb["tf"], "cto": tb["to"],
                "cksf": tb["ksf"], "ckcf": tb["kcf"], "ckso": tb["kso"],
                "ckco": tb["kco"], "ccm": tb["cmask"], "cid": tb["ident"],
                "cones": tb["ones"], "ciota": tb["iota16"],
            })
        res = run_bass_kernel_spmd(nc, in_maps, core_ids=list(range(8)))
    y = np.empty((B, T, D), dtype=np.float32)
    for core in range(8):
        b, r = core // 4, core % 4
        tb = host_tables(r)
        y[b][tb["own_rows"]] = res.results[core]["y"]
    if _dbg:
        return y, res
    return y


# version 03
# speedup vs baseline: 1.0984x; 1.0984x over previous
import contextlib
import numpy as np
import ml_dtypes
import concourse.bass as bass
import concourse.mybir as mybir
from concourse.bass_utils import run_bass_kernel_spmd

F32 = mybir.dt.float32
BF16 = mybir.dt.bfloat16
AF = mybir.ActivationFunctionType
ALU = mybir.AluOpType
AX = mybir.AxisListType
BFD = ml_dtypes.bfloat16

B, T, D = 2, 4096, 2048
DH = 128
NFF = 8192
KTOP = 819
SCALE = float(DH ** -0.5)
EPS = 1e-5
NEG = -1e30
NT_OWN = 8
NTILE_SEQ = 32
DBLK = 16
UCH = 64
GRP = 256

CW_C1 = float(np.float32(int(np.float64(2 * np.pi) * 2**9) / 2**9))
CW_C2 = float(np.float32(int((np.float64(2 * np.pi) - CW_C1) * 2**22) / 2**22))
CW_C3 = float(np.float32(np.float64(2 * np.pi) - CW_C1 - CW_C2))
HPI = float(np.float32(np.pi / 2))


class Rec:
    ENGS = ("t", "v", "s", "g", "y")

    def __init__(self, nc, ctx):
        self.nc = nc
        self.ctx = ctx
        self.sem = {e: ctx.enter_context(nc.semaphore(f"es_{e}")) for e in self.ENGS}
        self.cnt = {e: 0 for e in self.ENGS}
        self.ops = {e: [] for e in self.ENGS}
        self.last_w = {}
        self.readers = {}
        self.water = {e: {} for e in self.ENGS}
        self.max_wait = {}
        self.dsem = {}
        self.dcnt = {}
        self._dsem_id = 0

    def _get_dsem(self, key):
        if key not in self.dsem:
            self._dsem_id += 1
            self.dsem[key] = self.ctx.enter_context(
                self.nc.semaphore(f"dq{self._dsem_id}"))
            self.dcnt[key] = 0
        return self.dsem[key]

    def op(self, e, fn, reads=(), writes=(), dma=False, mark=True):
        need = []
        for r in reads:
            w = self.last_w.get(r)
            if w is not None:
                need.append(w)
        for w_ in writes:
            w = self.last_w.get(w_)
            if w is not None:
                need.append(w)
            rd = self.readers.get(w_)
            if rd:
                for k, c in rd.items():
                    need.append((k[0], k[1], c))
        deps = {}
        for kind, peng, c in need:
            if kind == "e" and peng == e and e == "t" and not dma:
                continue
            key = (kind, peng)
            if self.water[e].get(key, 0) >= c:
                continue
            deps[key] = max(deps.get(key, 0), c)
        if dma:
            semkey = writes[0] if writes else f"__pool_{e}"
            self._get_dsem(semkey)
            self.dcnt[semkey] += 1
            mykind, myval = semkey, 16 * self.dcnt[semkey]
        elif mark:
            self.cnt[e] += 1
            mykind, myval = "e", self.cnt[e]
        else:
            mykind, myval = "e", self.cnt[e] + 1
        waits = []
        for (kind, peng), c in deps.items():
            sem = self.sem[peng] if kind == "e" else self.dsem[kind]
            waits.append((sem, c))
            self.water[e][(kind, peng)] = c
            self.max_wait[(kind, peng)] = max(self.max_wait.get((kind, peng), 0), c)
        msem = self.dsem[mykind] if dma else (self.sem[e] if mark else None)
        minc = 16 if dma else 1

        def closure(eng, waits=waits, fn=fn, msem=msem, minc=minc):
            for sem, c in waits:
                eng.wait_ge(sem, c)
            instr = fn(eng)
            if msem is not None:
                instr.then_inc(msem, minc)
        self.ops[e].append(closure)
        myk = (mykind, e)
        for r in reads:
            d = self.readers.setdefault(r, {})
            d[myk] = max(d.get(myk, 0), myval)
        for w_ in writes:
            self.last_w[w_] = (mykind, e, myval)
            self.readers[w_] = {}

    def wait_all(self, e):
        waits = []
        for pe in self.ENGS:
            if pe != e and self.cnt[pe] and self.water[e].get(("e", pe), 0) < self.cnt[pe]:
                waits.append((self.sem[pe], self.cnt[pe]))
                self.water[e][("e", pe)] = self.cnt[pe]
                self.max_wait[("e", pe)] = max(self.max_wait.get(("e", pe), 0), self.cnt[pe])
        for sk, n in self.dcnt.items():
            if n and self.water[e].get((sk, "*"), 0) < 16 * n:
                waits.append((self.dsem[sk], 16 * n))
                self.water[e][(sk, "*")] = 16 * n
                self.max_wait[(sk, "*")] = max(self.max_wait.get((sk, "*"), 0), 16 * n)

        def closure(eng, waits=waits):
            for sem, c in waits:
                eng.wait_ge(sem, c)
        self.ops[e].append(closure)

    def barrier(self):
        for e in self.ENGS:
            self.wait_all(e)

    def check(self):
        for (kind, peng), v in self.max_wait.items():
            if kind == "e":
                avail = self.cnt[peng]
            else:
                avail = 16 * self.dcnt.get(kind, 0)
            assert v <= avail, f"wait {v} on {kind}/{peng}, only {avail} will arrive"

    def finalize(self, block):
        self.check()
        for e, meth in (("y", block.sync), ("t", block.tensor), ("v", block.vector),
                        ("s", block.scalar), ("g", block.gpsimd)):
            ops = self.ops[e]

            def body(eng, ops=ops):
                for f in ops:
                    f(eng)
            meth(body)


def host_tables(r):
    TWO_PI = np.float64(2 * np.pi)
    inv_freq = (1.0 / (1e6 ** (np.arange(DH // 2) / (DH // 2)))).astype(np.float64)
    f128 = np.concatenate([inv_freq, inv_freq])
    fhi = f128.astype(np.float32)
    flo = (f128 - fhi.astype(np.float64)).astype(np.float32)
    own_rows = np.concatenate([np.arange(128 * (4 * j + r), 128 * (4 * j + r) + 128)
                               for j in range(NT_OWN)])
    t_full = np.arange(T, dtype=np.float32).reshape(1, T)
    t_own = own_rows.astype(np.float32).reshape(1, 1024)

    def ktabs(tv):
        ang = np.outer(f128, tv.astype(np.float64).ravel())
        return (np.rint(ang / TWO_PI).astype(np.float32),
                np.rint((ang + np.pi / 2) / TWO_PI).astype(np.float32))
    ksf, kcf = ktabs(t_full)
    kso, kco = ktabs(t_own)
    i = np.arange(128).reshape(128, 1)
    c = np.arange(512).reshape(1, 512)
    cmask = np.where(c <= 128 * r + i, 0.0, NEG).astype(np.float32)
    return dict(fhi=fhi.reshape(1, 128), flo=flo.reshape(1, 128),
                tf=t_full, to=t_own, ksf=ksf, kcf=kcf, kso=kso, kco=kco,
                cmask=cmask, own_rows=own_rows,
                ident=np.eye(128, dtype=BFD),
                ones=np.ones((1, 128), dtype=np.float32),
                iota16=np.broadcast_to(np.arange(1.0, 17.0, dtype=np.float32),
                                       (128, 16)).copy())


BGSER = False


def build_program(ctx, dbg=False, dbg_lite=False):
    nc = bass.Bass("TRN2", target_bir_lowering=False)
    E = ctx.enter_context

    def dram(n, shp, dt, kind="ExternalInput"):
        return nc.dram_tensor(n, shp, dt, kind=kind)
    d_xf = dram("xf", [T, D], BF16)
    d_xo = dram("xo", [1024, D], BF16)
    d_wq = dram("wq", [D, D], BF16)
    d_wk = dram("wk", [D, 512], BF16)
    d_wv = dram("wv", [D, 512], BF16)
    d_wo = dram("wo", [D, D], BF16)
    d_bqc = dram("bqc", [128, 16], BF16)
    d_bkc = dram("bkc", [128, 4], BF16)
    d_bvr = dram("bvr", [1, 512], BF16)
    d_wsyn = dram("wsyn", [NFF, D], F32)
    d_wout = dram("wout", [D, NFF], F32)
    d_bsyn = dram("bsyn", [1, NFF], F32)
    d_bout = dram("bout", [1, D], F32)
    d_gam = dram("gam", [1, D], F32)
    d_bet = dram("bet", [1, D], F32)
    d_fhi = dram("cfhi", [1, 128], F32)
    d_flo = dram("cflo", [1, 128], F32)
    d_tf = dram("ctf", [1, T], F32)
    d_to = dram("cto", [1, 1024], F32)
    d_ksf = dram("cksf", [128, T], F32)
    d_kcf = dram("ckcf", [128, T], F32)
    d_kso = dram("ckso", [128, 1024], F32)
    d_kco = dram("ckco", [128, 1024], F32)
    d_cm = dram("ccm", [128, 512], F32)
    d_id = dram("cid", [128, 128], BF16)
    d_ones = dram("cones", [1, 128], F32)
    d_iota = dram("ciota", [128, 16], F32)
    d_y = dram("y", [1024, D], F32, kind="ExternalOutput")
    d_tsyn = dram("scr_tsyn", [DBLK, 128, NFF], BF16, kind="Internal")
    d_tout = dram("scr_tout", [UCH, 128, D], BF16, kind="Internal")
    d_x1 = dram("scr_x1", [1024, D], BF16, kind="Internal")
    d_bsr = dram("scr_bsr", [1, NFF], BF16, kind="Internal")
    dbg_t = {}
    if dbg:
        for name, shp, dt in [("dxln", [T, D], BF16), ("dv", [T, 512], BF16),
                              ("dkT", [512, T], BF16), ("dqT", [D, 1024], BF16),
                              ("dao", [1024, D], F32), ("dx1", [1024, D], BF16),
                              ("dz", [1024, NFF], F32), ("dvk", [1024, 4], F32),
                              ("dhid", [1024, NFF], F32),
                              ("dcos", [128, T], BF16), ("dsin", [128, T], BF16),
                              ("dffn", [1024, D], F32), ("dqT2", [D, 512], BF16),
                              ("dprobe", [8, 128, 512], BF16),
                              ("dkT2", [512, T], BF16), ("dv2", [T, 512], BF16)]:
            dbg_t[name] = dram(name, shp, dt, kind="ExternalOutput")

    LITE = [dbg_lite]
    if dbg_lite and not dbg:
        dbg_t["dx1f"] = dram("dx1f", [1024, D], BF16, kind="ExternalOutput")
        dbg_t["dvk"] = dram("dvk", [1024, 4], F32, kind="ExternalOutput")
        dbg_t["dbis"] = dram("dbis", [1024, NFF], BF16, kind="ExternalOutput")
        dbg_t["dcst"] = dram("dcst", [128, 2 * D + 2], F32, kind="ExternalOutput")
        dbg_t["dzl"] = dram("dzl", [128, NFF], F32, kind="ExternalOutput")
        dbg_t["dfl"] = dram("dfl", [128, D], F32, kind="ExternalOutput")
        dbg_t["dtout"] = dram("dtout", [UCH, 128, D], BF16, kind="ExternalOutput")
        dbg_t["dpart"] = dram("dpart", [128, 128], F32, kind="ExternalOutput")
    elif not dbg:
        LITE = [False]
    R = Rec(nc, ctx)
    sb = lambda n, shp, dt: E(nc.sbuf_tensor(n, shp, dt))

    p_a = E(nc.psum_tensor("pA", [128, 512], F32))
    p_b = E(nc.psum_tensor("pB", [128, 512], F32))
    p_c = E(nc.psum_tensor("pC", [128, 512], F32))
    p_t = E(nc.psum_tensor("pT", [128, 1024], BF16))
    p_mm = E(nc.psum_tensor("pMM", [128, 2048], F32))

    s_id = sb("c_id", [128, 128], BF16)
    s_ones = sb("c_ones", [1, 128], F32)
    s_onec = sb("c_onec", [128, 1], F32)
    s_iota = sb("c_iota", [128, 16], F32)
    s_eps = sb("c_eps", [128, 1], F32)
    s_fhi = sb("c_fhi", [1, 128], F32)
    s_flo = sb("c_flo", [1, 128], F32)
    s_bqc = sb("bqcs", [128, 16], F32)
    s_bkc = sb("bkcs", [128, 4], F32)
    s_row = sb("rowbuf", [1, 512], F32)
    s_cm = sb("c_cm", [128, 512], F32)
    s_m3 = sb("m3", [128, NT_OWN, 2], F32)
    s_part = sb("part", [128, 128], F32)
    _packed = ("mu", "sq", "var", "rstd", "nmr", "m", "negm", "rs", "pr",
               "lo", "hi", "mid", "chi", "cn", "cnb", "vk", "r",
               "taus", "tauo", "ntaus", "ntauo", "nss", "nso")
    sm = {n: sb("sm_" + n, [128, 1], F32) for n in _packed}
    for n in ("t0", "t1", "t2"):
        sm[n] = sb("sm_" + n, [128, 1], F32)
    s_tau_s, s_tau_o = sm["taus"], sm["tauo"]
    s_ntau_s, s_ntau_o = sm["ntaus"], sm["ntauo"]
    s_nss, s_nso = sm["nss"], sm["nso"]
    s_sqscr = sb("sqscr", [128, D], F32)
    s_pred = sb("predu8", [128, 1], mybir.dt.uint8)

    def dma(e, out, in_, reads=(), writes=()):
        R.op(e, lambda g: g.dma_start(out, in_), reads=reads, writes=writes, dma=True)

    def mm(out_ap, lhs_ap, rhs_ap, reads, pw, start, stop, mark=None):
        R.op("t", lambda t: t.matmul(out_ap, lhs_ap, rhs_ap, start=start, stop=stop),
             reads=list(reads), writes=[pw], mark=(stop if mark is None else mark))

    def tr(out_ap, in_ap, reads, mark):
        R.op("t", lambda t: t.transpose(out_ap, in_ap, s_id[:, :]),
             reads=list(reads) + ["c_id"], writes=["pT"], mark=mark)

    def v_op(fn, reads, writes):
        R.op("v", fn, reads=reads, writes=writes)

    def s_op(fn, reads, writes):
        R.op("s", fn, reads=reads, writes=writes)

    def ln_tile(xap, xres, outap, outres, n=D, save=None):
        v_op(lambda v: v.tensor_reduce(sm["mu"][:, :], xap, axis=AX.X, op=ALU.add),
             [xres], ["mu"])
        s_op(lambda s: s.activation(s_sqscr[:, :n], xap, AF.Square, accum_out=sm["sq"][:, :]),
             [xres], ["sqscr", "sq"])
        v_op(lambda v: v.tensor_scalar(sm["mu"][:, :], sm["mu"][:, :], 1.0 / n, None,
                                       op0=ALU.mult), ["mu"], ["mu"])
        v_op(lambda v: v.tensor_mul(sm["var"][:, :], sm["mu"][:, :], sm["mu"][:, :]),
             ["mu"], ["var"])
        v_op(lambda v: v.scalar_tensor_tensor(sm["var"][:, :], sm["sq"][:, :], 1.0 / n,
                                              sm["var"][:, :], op0=ALU.mult, op1=ALU.subtract),
             ["sq", "var"], ["var"])
        s_op(lambda s: s.activation(sm["rstd"][:, :], sm["var"][:, :], AF.Ln, bias=s_eps[:, :]),
             ["var", "c_eps"], ["rstd"])
        s_op(lambda s: s.activation(sm["rstd"][:, :], sm["rstd"][:, :], AF.Exp, scale=-0.5),
             ["rstd"], ["rstd"])
        if outap is not None:
            v_op(lambda v: v.tensor_scalar(outap, xap, sm["mu"][:, :], sm["rstd"][:, :],
                                           op0=ALU.subtract, op1=ALU.mult),
                 [xres, "mu", "rstd"], [outres])
        if save is not None:
            ap_, res_ = save
            v_op(lambda v: v.tensor_copy(ap_[:, 0:1], sm["mu"][:, :]), ["mu"], [res_])
            v_op(lambda v: v.tensor_copy(ap_[:, 1:2], sm["rstd"][:, :]), ["rstd"], [res_])

    def transpose16(src_ap_fn, srcres, dst_ap_fn, dstres, nch=DBLK):
        for c8 in range(0, nch, 8):
            for k in range(8):
                tr(p_t[:, k * 128:(k + 1) * 128], src_ap_fn(c8 + k), [srcres], mark=(k == 7))
            for k in range(8):
                v_op(lambda v, c8=c8, k=k: v.tensor_copy(dst_ap_fn(c8 + k),
                                                         p_t[:, k * 128:(k + 1) * 128]),
                     ["pT"], [dstres])

    def bcast_dram(dten, dst_ap, dst_res, n, off=0):
        for i in range(0, n, 512):
            w = min(512, n - i)
            dma("y", s_row[:1, :w], dten.ap()[:1, off + i:off + i + w],
                reads=["rowbuf"], writes=["rowbuf"])
            mm(p_c[:, :w], s_ones[:1, :], s_row[:1, :w],
               ["rowbuf", "c_ones"], "pC", True, True)
            v_op(lambda v, i=i, w=w: v.tensor_copy(dst_ap[:, i:i + w], p_c[:, :w]),
                 ["pC"], [dst_res])

    def scalar_stat(partial_slice, pres, nsdst, nsres, taudst, taures):
        v_op(lambda v: v.tensor_reduce(sm["t0"][:, :], partial_slice, axis=AX.X, op=ALU.add),
             [pres], ["t0"])
        mm(p_c[:1, 0:1], sm["t0"][:, :1], s_onec[:, :1], ["t0", "c_onec"], "pC", True, True)
        v_op(lambda v: v.tensor_scalar(sm["t1"][:1, :1], p_c[:1, 0:1], 1.0 / (NFF * D), None,
                                       op0=ALU.mult), ["pC"], ["t1"])
        v_op(lambda v: v.tensor_scalar(sm["t2"][:1, :1], sm["t1"][:1, :1], 1e-8, None,
                                       op0=ALU.add), ["t1"], ["t2"])
        mm(p_c[:, 0:1], s_ones[:1, :], sm["t1"][:1, :1], ["t1", "c_ones"], "pC", True, True)
        v_op(lambda v: v.tensor_scalar(nsdst[:, :], p_c[:, 0:1], -1.0, None, op0=ALU.mult),
             ["pC"], [nsres])
        mm(p_c[:, 1:2], s_ones[:1, :], sm["t2"][:1, :1], ["t2", "c_ones"], "pC", True, True)
        v_op(lambda v, taudst=taudst: v.tensor_scalar(taudst[:, :], p_c[:, 1:2], 0.5, None,
                                                      op0=ALU.mult), ["pC"], [taures])

    dma("y", s_id[:, :], d_id.ap(), writes=["c_id"])
    dma("y", s_ones[:1, :], d_ones.ap(), writes=["c_ones"])
    dma("y", s_iota[:, :], d_iota.ap(), writes=["c_iota"])
    dma("y", s_cm[:, :], d_cm.ap(), writes=["c_cm"])
    dma("y", s_fhi[:1, :], d_fhi.ap(), writes=["fhi"])
    dma("y", s_flo[:1, :], d_flo.ap(), writes=["flo"])
    R.op("v", lambda v: v.memset(s_eps[:, :], EPS), writes=["c_eps"])
    R.op("v", lambda v: v.memset(s_onec[:, :], 1.0), writes=["c_onec"])

    with contextlib.ExitStack() as pa:
        sba = lambda n, shp, dt: pa.enter_context(nc.sbuf_tensor(n, shp, dt))
        s_cosf = sba("cosf", [128, T], BF16)
        s_sinf = sba("sinf", [128, T], BF16)
        s_coso = sba("coso", [128, 1024], BF16)
        s_sino = sba("sino", [128, 1024], BF16)
        s_kT = sba("kT", [128, 4, T], BF16)
        s_v = sba("vv", [128, NTILE_SEQ, 512], BF16)
        s_wbig = sba("wbig", [128, D], F32)
        s_bvb = sba("bvb", [128, 512], F32)
        s_tst = sba("tst", [128, 2048], BF16)
        s_stg = sba("stg", [128, 8, 128], BF16)

        with contextlib.ExitStack() as p0:
            sb0 = lambda n, shp, dt: p0.enter_context(nc.sbuf_tensor(n, shp, dt))
            t_bq = sb0("t_bq", [128, 16], BF16)
            t_bk = sb0("t_bk", [128, 4], BF16)
            t_bv = sb0("t_bv", [1, 512], BF16)
            t_bvf = sb0("t_bvf", [1, 512], F32)
            dma("y", t_bq[:, :], d_bqc.ap(), writes=["tbq"])
            dma("y", t_bk[:, :], d_bkc.ap(), writes=["tbk"])
            dma("y", t_bv[:1, :], d_bvr.ap(), writes=["tbv"])
            v_op(lambda v: v.tensor_copy(s_bqc[:, :], t_bq[:, :]), ["tbq"], ["bqcs"])
            v_op(lambda v: v.tensor_copy(s_bkc[:, :], t_bk[:, :]), ["tbk"], ["bkcs"])
            v_op(lambda v: v.tensor_copy(t_bvf[:1, :], t_bv[:1, :]), ["tbv"], ["tbvf"])
            mm(p_c[:, :], s_ones[:1, :], t_bvf[:1, :], ["tbvf", "c_ones"], "pC", True, True)
            v_op(lambda v: v.tensor_copy(s_bvb[:, :], p_c[:, :]), ["pC"], ["bvb"])

            s_ang = sb0("w_ang", [128, T], F32)
            s_th = sb0("w_th", [128, T], F32)
            s_k1 = sb0("w_k1", [128, T], F32)
            s_k2 = sb0("w_k2", [128, T], F32)
            s_trw = sb0("w_trw", [1, T], F32)

            def trig(ks_d, kc_d, trow_d, n, cosb, sinb, cres, sres):
                dma("y", s_k1[:, :n], ks_d.ap(), reads=["k1"], writes=["k1"])
                dma("y", s_k2[:, :n], kc_d.ap(), reads=["k2"], writes=["k2"])
                dma("y", s_trw[:1, :n], trow_d.ap(), reads=["trow"], writes=["trow"])
                for i in range(n // 512):
                    mm(p_c[:, :], s_fhi[:1, :], s_trw[:1, i * 512:(i + 1) * 512],
                       ["fhi", "trow"], "pC", True, False)
                    mm(p_c[:, :], s_flo[:1, :], s_trw[:1, i * 512:(i + 1) * 512],
                       ["flo", "trow"], "pC", False, True)
                    v_op(lambda v, i=i: v.tensor_copy(s_ang[:, i * 512:(i + 1) * 512],
                                                      p_c[:, :]), ["pC"], ["ang"])
                first = [True]
                for c_ in (CW_C1, CW_C2, CW_C3):
                    sap = s_ang if first[0] else s_th
                    srs = "ang" if first[0] else "th"
                    v_op(lambda v, c_=c_, sap=sap: v.scalar_tensor_tensor(
                            s_th[:, :n], s_k1[:, :n], -c_, sap[:, :n],
                            op0=ALU.mult, op1=ALU.add), ["k1", srs], ["th"])
                    first[0] = False
                s_op(lambda s: s.activation(sinb[:, :n], s_th[:, :n], AF.Sin), ["th"], [sres])
                v_op(lambda v: v.tensor_scalar(s_th[:, :n], s_ang[:, :n], HPI, None,
                                               op0=ALU.add), ["ang"], ["th"])
                for c_ in (CW_C1, CW_C2, CW_C3):
                    v_op(lambda v, c_=c_: v.scalar_tensor_tensor(
                            s_th[:, :n], s_k2[:, :n], -c_, s_th[:, :n],
                            op0=ALU.mult, op1=ALU.add), ["k2", "th"], ["th"])
                s_op(lambda s: s.activation(cosb[:, :n], s_th[:, :n], AF.Sin), ["th"], [cres])

            trig(d_ksf, d_kcf, d_tf, T, s_cosf, s_sinf, "cosF", "sinF")
            trig(d_kso, d_kco, d_to, 1024, s_coso, s_sino, "cosO", "sinO")
            if dbg:
                dma("g", dbg_t["dcos"].ap(), s_cosf[:, :], reads=["cosF"])
                dma("g", dbg_t["dsin"].ap(), s_sinf[:, :], reads=["sinF"])
            R.barrier()

        def mk_wsyn_scale(i):
            def f1():
                dma("y", s_wbig[:, :], d_wsyn.ap()[i * 128:(i + 1) * 128, :],
                    reads=["wbig"], writes=["wbig"])

            def f2():
                s_op(lambda s: s.activation(s_sqscr[:, :], s_wbig[:, :], AF.Abs,
                                            accum_out=s_part[:, i:i + 1]),
                     ["wbig"], ["sqscr", "part"])
            return [f1, f2]

        def mk_wout_scale(i, h):
            def f1():
                dma("y", s_wbig[:, :],
                    d_wout.ap()[i * 128:(i + 1) * 128, h * 2048:(h + 1) * 2048],
                    reads=["wbig"], writes=["wbig"])

            def f2():
                v_op(lambda v: v.tensor_reduce(s_part[:, 64 + i * 4 + h:64 + i * 4 + h + 1],
                                               s_wbig[:, :], axis=AX.X, op=ALU.add,
                                               apply_absolute_value=True),
                     ["wbig"], ["part2"])
            return [f1, f2]

        def mk_tsyn(i):
            def f1():
                dma("g", s_wbig[:, :], d_wsyn.ap()[i * 128:(i + 1) * 128, :],
                    reads=["wbig"], writes=["wbig"])

            def f2():
                v_op(lambda v: v.tensor_scalar(s_tst[:, :D], s_wbig[:, :], s_tau_s[:, :],
                                               None, op0=ALU.is_gt),
                     ["wbig", "taus"], ["tst"])
                v_op(lambda v: v.scalar_tensor_tensor(s_tst[:, :D], s_wbig[:, :],
                                                      s_ntau_s[:, :], s_tst[:, :D],
                                                      op0=ALU.is_lt, op1=ALU.subtract),
                     ["wbig", "ntaus", "tst"], ["tst"])

            def f3():
                for c8 in range(0, DBLK, 8):
                    for k in range(8):
                        tr(p_t[:, k * 128:(k + 1) * 128],
                           s_tst[:, (c8 + k) * 128:(c8 + k + 1) * 128], ["tst"], mark=(k == 7))
                    for k in range(8):
                        v_op(lambda v, k=k: v.tensor_copy(s_stg[:, k, :],
                                                          p_t[:, k * 128:(k + 1) * 128]),
                             ["pT"], ["stg"])
                    dma("g", d_tsyn.ap()[c8:c8 + 8, :, i * 128:(i + 1) * 128]
                            .rearrange("c p u -> p c u"),
                        s_stg[:, :, :], reads=["stg"], writes=["TSYN"])
            return [f1, f2, f3]

        def mk_tout(i, q):
            def f1():
                dma("g", s_wbig[:, :],
                    d_wout.ap()[i * 128:(i + 1) * 128, q * 2048:(q + 1) * 2048],
                    reads=["wbig"], writes=["wbig"])

            def f2():
                v_op(lambda v: v.tensor_scalar(s_tst[:, :], s_wbig[:, :], s_tau_o[:, :],
                                               None, op0=ALU.is_gt),
                     ["wbig", "tauo"], ["tst"])
                v_op(lambda v: v.scalar_tensor_tensor(s_tst[:, :], s_wbig[:, :],
                                                      s_ntau_o[:, :], s_tst[:, :],
                                                      op0=ALU.is_lt, op1=ALU.subtract),
                     ["wbig", "ntauo", "tst"], ["tst"])

            def f3():
                for u8 in range(0, 16, 8):
                    for k in range(8):
                        tr(p_t[:, k * 128:(k + 1) * 128],
                           s_tst[:, (u8 + k) * 128:(u8 + k + 1) * 128], ["tst"], mark=(k == 7))
                    for k in range(8):
                        v_op(lambda v, k=k: v.tensor_copy(s_stg[:, k, :],
                                                          p_t[:, k * 128:(k + 1) * 128]),
                             ["pT"], ["stg"])
                    dma("g", d_tout.ap()[q * 16 + u8:q * 16 + u8 + 8, :,
                                         i * 128:(i + 1) * 128]
                            .rearrange("u p m -> p u m"),
                        s_stg[:, :, :], reads=["stg"], writes=["TOUT"])
            return [f1, f2, f3]

        bgq = []
        bqi = [0]

        def pump(n):
            for _ in range(n):
                if bqi[0] < len(bgq):
                    bgq[bqi[0]]()
                    bqi[0] += 1
                    if BGSER:
                        R.barrier()

        def pump_all():
            pump(len(bgq) - bqi[0])

        def set_bg(tasks):
            pump_all()
            bgq.clear()
            bqi[0] = 0
            bgq.extend(tasks)

        tasks = []
        for i in range(UCH):
            tasks += mk_wsyn_scale(i)
        set_bg(tasks)
        with contextlib.ExitStack() as p2:
            sb2 = lambda n, shp, dt: p2.enter_context(nc.sbuf_tensor(n, shp, dt))
            s_xin = [sb2(f"xin{i}", [128, D], BF16) for i in range(2)]
            s_xln = [sb2(f"xln{i}", [128, D], BF16) for i in range(2)]
            s_xlT = sb2("xlT", [128, DBLK, GRP], BF16)
            s_wk = sb2("wks", [128, DBLK, 512], BF16)
            s_wv = sb2("wvs", [128, DBLK, 512], BF16)
            s_rt1 = sb2("rt1", [128, GRP], BF16)
            s_rt2 = sb2("rt2", [128, GRP], BF16)
            s_rt3 = sb2("rt3", [128, GRP], BF16)

            dma("y", s_wk[:, :, :], d_wk.ap().rearrange("(c p) f -> p c f", p=128),
                writes=["wks"])
            dma("y", s_wv[:, :, :], d_wv.ap().rearrange("(c p) f -> p c f", p=128),
                writes=["wvs"])

            def rope(dst_ap, dst_res, src_ap, src_res, c0):
                v_op(lambda v: v.tensor_scalar(s_rt2[0:64, :], src_ap[64:128, :], -1.0, None,
                                               op0=ALU.mult), [src_res], ["rt2"])
                v_op(lambda v: v.tensor_copy(s_rt2[64:128, :], src_ap[0:64, :]),
                     [src_res], ["rt2"])
                v_op(lambda v: v.tensor_mul(s_rt3[:, :], src_ap, s_cosf[:, c0:c0 + GRP]),
                     [src_res, "cosF"], ["rt3"])
                v_op(lambda v: v.tensor_mul(s_rt2[:, :], s_rt2[:, :], s_sinf[:, c0:c0 + GRP]),
                     ["rt2", "sinF"], ["rt2"])
                v_op(lambda v: v.tensor_add(dst_ap, s_rt3[:, :], s_rt2[:, :]),
                     ["rt3", "rt2"], [dst_res])

            for g in range(T // GRP):
                for tt_ in range(2):
                    it = g * 2 + tt_
                    pb = it % 2
                    dma("y", s_xin[pb][:, :], d_xf.ap()[it * 128:(it + 1) * 128, :],
                        reads=[f"xin{pb}"], writes=[f"xin{pb}"])
                    ln_tile(s_xin[pb][:, :], f"xin{pb}", s_xln[pb][:, :], f"xln{pb}")
                    if dbg:
                        dma("g", dbg_t["dxln"].ap()[it * 128:(it + 1) * 128, :],
                            s_xln[pb][:, :], reads=[f"xln{pb}"])
                    transpose16(lambda c, pb=pb: s_xln[pb][:, c * 128:(c + 1) * 128],
                                f"xln{pb}",
                                lambda c, tt_=tt_: s_xlT[:, c, tt_ * 128:(tt_ + 1) * 128],
                                "xlT")
                    for c in range(DBLK):
                        mm(p_b[:, :], s_xlT[:, c, tt_ * 128:(tt_ + 1) * 128], s_wv[:, c, :],
                           ["xlT", "wvs"], "pB", c == 0, c == DBLK - 1)
                    v_op(lambda v, it=it: v.tensor_add(s_v[:, it, :], p_b[:, :], s_bvb[:, :]),
                         ["pB", "bvb"], [f"vt{it}"])
                    if dbg:
                        dma("g", dbg_t["dv"].ap()[it * 128:(it + 1) * 128, :],
                            s_v[:, it, :], reads=[f"vt{it}"])
                    pump(4)
                for f in range(4):
                    pp, ppr = (p_a, "pA") if f % 2 == 0 else (p_b, "pB")
                    for c in range(DBLK):
                        mm(pp[:, :GRP], s_wk[:, c, f * 128:(f + 1) * 128], s_xlT[:, c, :],
                           ["wks", "xlT"], ppr, c == 0, c == DBLK - 1)
                    s_op(lambda s, pp=pp, f=f: s.activation(s_rt1[:, :], pp[:, :GRP],
                                                            AF.Identity,
                                                            bias=s_bkc[:, f:f + 1]),
                         [ppr, "bkcs"], ["rt1"])
                    rope(s_kT[:, f, g * GRP:(g + 1) * GRP], f"kT{f}", s_rt1[:, :], "rt1",
                         g * GRP)
                pump(2)
            pump_all()
            if dbg:
                for f in range(4):
                    dma("g", dbg_t["dkT"].ap()[f * 128:(f + 1) * 128, :], s_kT[:, f, :],
                        reads=[f"kT{f}"])
            scalar_stat(s_part[:, 0:64], "part", s_nss, "nss", s_tau_s, "taus")
            v_op(lambda v: v.tensor_scalar(s_ntau_s[:, :], s_tau_s[:, :], -1.0, None,
                                           op0=ALU.mult), ["taus"], ["ntaus"])
            R.barrier()

        with contextlib.ExitStack() as p3:
            sb3 = lambda n, shp, dt: p3.enter_context(nc.sbuf_tensor(n, shp, dt))
            s_sc = sb3("scrow", [128, T], BF16)
            s_pp = [sb3(f"prow{i}", [128, T], BF16) for i in range(2)]
            s_ring = [sb3(f"ring{i}", [128, 256], BF16) for i in range(2)]
            s_avt = sb3("avt", [128, 16, 512], BF16)
            s_ao = sb3("ao", [128, D], F32)
            s_xot = sb3("xot", [128, D], BF16)
            s_qrt1 = sb3("qrt1", [128, 512], BF16)
            s_qrt2 = sb3("qrt2", [128, 512], BF16)
            s_qrt3 = sb3("qrt3", [128, 512], BF16)

            def rope_q(dst_ap, dst_res, src_ap, src_res, batch):
                c0 = batch * 512
                v_op(lambda v: v.tensor_scalar(s_qrt2[0:64, :], src_ap[64:128, :], -1.0,
                                               None, op0=ALU.mult), [src_res], ["qrt2"])
                v_op(lambda v: v.tensor_copy(s_qrt2[64:128, :], src_ap[0:64, :]),
                     [src_res], ["qrt2"])
                v_op(lambda v: v.tensor_mul(s_qrt3[:, :], src_ap, s_coso[:, c0:c0 + 512]),
                     [src_res, "cosO"], ["qrt3"])
                v_op(lambda v: v.tensor_mul(s_qrt2[:, :], s_qrt2[:, :],
                                            s_sino[:, c0:c0 + 512]),
                     ["qrt2", "sinO"], ["qrt2"])
                v_op(lambda v: v.tensor_add(dst_ap, s_qrt3[:, :], s_qrt2[:, :]),
                     ["qrt3", "qrt2"], [dst_res])

            tasks = []
            for i in range(DBLK):
                for h in range(4):
                    tasks += mk_wout_scale(i, h)
            for i in range(UCH):
                tasks += mk_tsyn(i)
            set_bg(tasks)

            for batch in range(2):
                with contextlib.ExitStack() as pq:
                    s_qT = pq.enter_context(nc.sbuf_tensor(f"qT{batch}", [128, 16, 512],
                                                           BF16))
                    s_wqb = [pq.enter_context(nc.sbuf_tensor(f"wqb{batch}_0",
                                                             [128, DBLK, 128], BF16))]
                    s_xoT = pq.enter_context(nc.sbuf_tensor(f"xoT{batch}",
                                                            [128, DBLK, 512], BF16))
                    for jj in range(4):
                        j = batch * 4 + jj
                        dma("y", s_xot[:, :], d_xo.ap()[j * 128:(j + 1) * 128, :],
                            reads=["xot"], writes=["xot"])
                        ln_tile(s_xot[:, :], "xot", s_xot[:, :], "xot")
                        transpose16(lambda c: s_xot[:, c * 128:(c + 1) * 128], "xot",
                                    lambda c, jj=jj: s_xoT[:, c, jj * 128:(jj + 1) * 128],
                                    "xoT")
                        pump(3)
                    for f in range(16):
                        pbq = 0
                        dma("y", s_wqb[pbq][:, :, :],
                            d_wq.ap().rearrange("(c p) f -> p c f",
                                                p=128)[:, :, f * 128:(f + 1) * 128],
                            reads=[f"wqb{pbq}"], writes=[f"wqb{pbq}"])
                        pp, ppr = (p_a, "pA") if f % 2 == 0 else (p_b, "pB")
                        for c in range(DBLK):
                            mm(pp[:, :], s_wqb[pbq][:, c, :], s_xoT[:, c, :],
                               [f"wqb{pbq}", "xoT"], ppr, c == 0, c == DBLK - 1)
                        s_op(lambda s, pp=pp, f=f: s.activation(s_qrt1[:, :], pp[:, :],
                                                                AF.Identity,
                                                                bias=s_bqc[:, f:f + 1]),
                             [ppr, "bqcs"], ["qrt1"])
                        rope_q(s_qT[:, f, :], f"qT{f}", s_qrt1[:, :], "qrt1", batch)
                        pump(2)
                    if dbg:
                        for f in range(16):
                            dma("g", dbg_t["dqT"].ap()[f * 128:(f + 1) * 128,
                                                       batch * 512:(batch + 1) * 512],
                                s_qT[:, f, :], reads=[f"qT{f}"])
                    for h in range(16):
                        g_kv = h // 4
                        for sub in range(2):
                            for jj2 in range(2):
                                jj = sub * 2 + jj2
                                j = batch * 4 + jj
                                nkb = j + 1
                                R.op("v", lambda v: v.memset(sm["m"][:, :], -3e38),
                                     writes=["m"])
                                for kb in range(nkb):
                                    pp, ppr = (p_a, "pA") if kb % 2 == 0 else (p_b, "pB")
                                    mm(pp[:, :], s_qT[:, h, jj * 128:(jj + 1) * 128],
                                       s_kT[:, g_kv, kb * 512:(kb + 1) * 512],
                                       [f"qT{h}", f"kT{g_kv}"], ppr, True, True)
                                    if kb == nkb - 1:
                                        v_op(lambda v, pp=pp: v.tensor_add(
                                                pp[:, :], pp[:, :], s_cm[:, :]),
                                             [ppr, "c_cm"], [ppr])
                                    v_op(lambda v, pp=pp, kb=kb: v.tensor_scalar(
                                            s_sc[:, kb * 512:(kb + 1) * 512], pp[:, :],
                                            SCALE, None, op0=ALU.mult, op1=ALU.max,
                                            accum_out=sm["t0"][:, :]),
                                         [ppr], ["scrow", "t0"])
                                    v_op(lambda v: v.tensor_max(sm["m"][:, :], sm["m"][:, :],
                                                                sm["t0"][:, :]),
                                         ["m", "t0"], ["m"])
                                v_op(lambda v: v.tensor_scalar(sm["negm"][:, :],
                                                               sm["m"][:, :], -1.0, None,
                                                               op0=ALU.mult),
                                     ["m"], ["negm"])
                                s_op(lambda s, jj2=jj2, nkb=nkb: s.activation(
                                        s_pp[jj2][:, :nkb * 512], s_sc[:, :nkb * 512],
                                        AF.Exp, bias=sm["negm"][:, :],
                                        accum_out=sm["rs"][:, :]),
                                     ["scrow", "negm"], [f"prow{jj2}", "rs"])
                                v_op(lambda v: v.reciprocal(sm["pr"][:, :], sm["rs"][:, :]),
                                     ["rs"], ["pr"])
                                v_op(lambda v, jj2=jj2, nkb=nkb: v.tensor_scalar(
                                        s_pp[jj2][:, :nkb * 512], s_pp[jj2][:, :nkb * 512],
                                        sm["pr"][:, :], None, op0=ALU.mult),
                                     [f"prow{jj2}", "pr"], [f"prow{jj2}"])
                            j1 = batch * 4 + sub * 2 + 1
                            nkb128 = 4 * (j1 + 1)
                            for kb in range(nkb128):
                                rpb = kb % 2
                                for jj2 in range(2):
                                    j = batch * 4 + sub * 2 + jj2
                                    if kb < 4 * (j + 1):
                                        tr(p_t[:, jj2 * 128:(jj2 + 1) * 128],
                                           s_pp[jj2][:, kb * 128:(kb + 1) * 128],
                                           [f"prow{jj2}"], mark=True)
                                        v_op(lambda v, rpb=rpb, jj2=jj2: v.tensor_copy(
                                                s_ring[rpb][:, jj2 * 128:(jj2 + 1) * 128],
                                                p_t[:, jj2 * 128:(jj2 + 1) * 128]),
                                             ["pT"], [f"ring{rpb}"])
                                    else:
                                        R.op("v", lambda v, rpb=rpb, jj2=jj2: v.memset(
                                                s_ring[rpb][:, jj2 * 128:(jj2 + 1) * 128],
                                                0.0),
                                             writes=[f"ring{rpb}"])
                                mm(p_c[:, sub * 256:(sub + 1) * 256],
                                   s_v[:, kb, g_kv * 128:(g_kv + 1) * 128],
                                   s_ring[rpb][:, :],
                                   [f"vt{kb}", f"ring{rpb}"], "pC", kb == 0,
                                   kb == nkb128 - 1)
                            s_op(lambda s, sub=sub, h=h: s.activation(
                                    s_avt[:, h, sub * 256:(sub + 1) * 256],
                                    p_c[:, sub * 256:(sub + 1) * 256], AF.Identity),
                                 ["pC"], [f"avt{h}"])
                            if dbg and batch == 1 and h < 2:
                                idx = h * 2 + sub
                                dma("g", dbg_t["dprobe"].ap()[idx, :, :],
                                    s_qT[:, 0, :], reads=[f"qT0", f"avt{h}"])
                        pump(4)
                    if dbg and batch == 1:
                        for f in range(16):
                            dma("g", dbg_t["dqT2"].ap()[f * 128:(f + 1) * 128, :],
                                s_qT[:, f, :], reads=[f"qT{f}"])
                        for f in range(4):
                            dma("g", dbg_t["dkT2"].ap()[f * 128:(f + 1) * 128, :],
                                s_kT[:, f, :], reads=[f"kT{f}"])
                        for it_ in range(NTILE_SEQ):
                            dma("g", dbg_t["dv2"].ap()[it_ * 128:(it_ + 1) * 128, :],
                                s_v[:, it_, :], reads=[f"vt{it_}"])
                R.barrier()
                with contextlib.ExitStack() as po:
                    s_wob = [po.enter_context(nc.sbuf_tensor(f"wob{batch}_{i}", [128, D],
                                                             BF16)) for i in range(2)]
                    for jj in range(4):
                        j = batch * 4 + jj
                        for hh in range(16):
                            pbo = hh % 2
                            dma("y", s_wob[pbo][:, :], d_wo.ap()[hh * 128:(hh + 1) * 128, :],
                                reads=[f"wob{pbo}"], writes=[f"wob{pbo}"])
                            for db in range(4):
                                mm(p_mm[:, db * 512:(db + 1) * 512],
                                   s_avt[:, hh, jj * 128:(jj + 1) * 128],
                                   s_wob[pbo][:, db * 512:(db + 1) * 512],
                                   [f"avt{hh}", f"wob{pbo}"], "pMM", hh == 0, hh == 15,
                                   mark=(db == 3))
                        v_op(lambda v: v.tensor_copy(s_ao[:, :], p_mm[:, :]),
                             ["pMM"], ["ao"])
                        if dbg:
                            dma("g", dbg_t["dao"].ap()[j * 128:(j + 1) * 128, :],
                                s_ao[:, :], reads=["ao"])
                        ln_tile(s_ao[:, :], "ao", s_sqscr[:, :], "sqscr")
                        dma("y", s_xot[:, :], d_xo.ap()[j * 128:(j + 1) * 128, :],
                            reads=["xot"], writes=["xot"])
                        v_op(lambda v: v.tensor_add(s_xot[:, :], s_sqscr[:, :],
                                                    s_xot[:, :]),
                             ["sqscr", "xot"], ["xot"])
                        dma("g", d_x1.ap()[j * 128:(j + 1) * 128, :], s_xot[:, :],
                            reads=["xot"], writes=[f"x1_{j}"])
                        if dbg:
                            dma("g", dbg_t["dx1"].ap()[j * 128:(j + 1) * 128, :],
                                s_xot[:, :], reads=["xot"])
                        pump(6)
                R.barrier()
                if batch == 0:
                    pump_all()
                    if LITE[0]:
                        dma("g", dbg_t["dpart"].ap()[:, :], s_part[:, :],
                            reads=["part", "part2"])
                    scalar_stat(s_part[:, 64:128], "part2", s_nso, "nso", s_tau_o, "tauo")
                    v_op(lambda v: v.tensor_scalar(s_ntau_o[:, :], s_tau_o[:, :], -1.0,
                                                   None, op0=ALU.mult), ["tauo"], ["ntauo"])
                    tasks = []
                    for i in range(DBLK):
                        for q_ in range(4):
                            tasks += mk_tout(i, q_)
                    set_bg(tasks)
            pump_all()
            R.barrier()

    with contextlib.ExitStack() as p4:
        sb4 = lambda n, shp, dt: p4.enter_context(nc.sbuf_tensor(n, shp, dt))
        s_gg = sb4("ggam", [128, D], BF16)
        s_gbt = sb4("gbet", [128, D], BF16)
        s_1bf = sb4("onesbf", [1, 128], BF16)
        s_borr = sb4("borr", [1, D], BF16)

        R.op("v", lambda v: v.memset(s_1bf[:1, :], 1.0), writes=["onesbf"])
        with contextlib.ExitStack() as ptmp:
            s_t32 = ptmp.enter_context(nc.sbuf_tensor("t32row", [1, NFF], F32))
            s_tbf = ptmp.enter_context(nc.sbuf_tensor("tbfrow", [1, NFF], BF16))
            dma("y", s_t32[:1, :], d_bsyn.ap()[:1, :], reads=["t32row"], writes=["t32row"])
            v_op(lambda v: v.reciprocal(sm["t0"][:1, :1], s_nss[:1, :1]), ["nss"], ["t0"])
            v_op(lambda v: v.tensor_scalar(s_tbf[:1, :], s_t32[:1, :], sm["t0"][:1, :1],
                                           None, op0=ALU.mult), ["t32row", "t0"], ["tbfrow"])
            dma("g", d_bsr.ap()[:1, :], s_tbf[:1, :], reads=["tbfrow"], writes=["BSR"])
            dma("y", s_t32[:1, :D], d_bout.ap()[:1, :], reads=["t32row"],
                writes=["t32row"])
            v_op(lambda v: v.reciprocal(sm["t1"][:1, :1], s_nso[:1, :1]), ["nso"], ["t1"])
            v_op(lambda v: v.tensor_scalar(s_borr[:1, :], s_t32[:1, :D], sm["t1"][:1, :1],
                                           None, op0=ALU.mult), ["t32row", "t1"], ["borr"])
            bcast_dram(d_gam, s_gg, "ggam", D)
            bcast_dram(d_bet, s_gbt, "gbet", D)
            R.barrier()

        s_z = [sb4(f"z{i}", [128, NFF], F32) for i in range(2)]
        s_u = sb4("ubuf", [128, 2048], F32)
        s_A = sb4("Abuf", [128, NFF], BF16)
        s_Bb = sb4("Bbuf", [128, NFF], BF16)
        s_bis = sb4("bis", [128, NFF], BF16)
        s_junk = sb4("junk", [128, NFF // 2], BF16)
        s_x2T = sb4("x2T", [128, DBLK, 256], BF16)
        s_x2 = sb4("x2", [128, D], BF16)
        s_x1t = sb4("x1t", [128, D], BF16)
        s_tsn = sb4("tsn", [128, DBLK, 512], BF16)
        s_ton = [sb4(f"ton{i}", [128, D], BF16) for i in range(2)]
        s_hiT = [sb4(f"hiT{i}", [128, 128], BF16) for i in range(2)]
        s_loT = [sb4(f"loT{i}", [128, 128], BF16) for i in range(2)]
        s_out = sb4("outb", [128, D], F32)
        s_brow = sb4("brow", [1, 512], BF16)
        s_c64 = sb4("c64", [128, 64], F32)
        s_m16 = sb4("m16", [128, 16], F32)
        s_m8 = sb4("m8", [128, 8], F32)
        s_cand = sb4("cand", [128, 16], F32)

        def topk_tile(zres, zap, jtap=None):
            v_op(lambda v: v.tensor_scalar(s_bis[:, :], zap, 0.0, None, op0=ALU.add),
                 [zres], ["bis"])
            if LITE[0] and jtap is not None:
                dma("g", dbg_t["dbis"].ap()[jtap * 128:(jtap + 1) * 128, :],
                    s_bis[:, :], reads=["bis"])
            R.op("v", lambda v: v.memset(sm["lo"][:, :], -64.0), writes=["lo"])
            R.op("v", lambda v: v.memset(sm["hi"][:, :], 64.0), writes=["hi"])
            R.op("v", lambda v: v.memset(sm["chi"][:, :], 0.0), writes=["chi"])

            def mid_update():
                v_op(lambda v: v.tensor_add(sm["mid"][:, :], sm["lo"][:, :], sm["hi"][:, :]),
                     ["lo", "hi"], ["mid"])
                v_op(lambda v: v.tensor_scalar(sm["mid"][:, :], sm["mid"][:, :], 0.5, None,
                                               op0=ALU.mult), ["mid"], ["mid"])

            def branch(track):
                v_op(lambda v: v.tensor_scalar(s_pred[:, :], sm["cn"][:, :], float(KTOP),
                                               None, op0=ALU.is_ge), ["cn"], ["predu"])
                v_op(lambda v: v.select(sm["lo"][:, :], s_pred[:, :], sm["mid"][:, :],
                                        sm["lo"][:, :]), ["predu", "mid", "lo"], ["lo"])
                if track:
                    v_op(lambda v: v.select(sm["chi"][:, :], s_pred[:, :], sm["chi"][:, :],
                                            sm["cn"][:, :]), ["predu", "chi", "cn"], ["chi"])
                v_op(lambda v: v.select(sm["hi"][:, :], s_pred[:, :], sm["hi"][:, :],
                                        sm["mid"][:, :]), ["predu", "hi", "mid"], ["hi"])

            def count2(src_fn, srcres, thr, dst, dstres):
                v_op(lambda v: v.tensor_scalar(s_junk[:, :], src_fn(0), sm[thr][:, :],
                                               None, op0=ALU.is_ge, op1=ALU.add,
                                               accum_out=dst),
                     [srcres, thr], ["junk", dstres])
                v_op(lambda v: v.tensor_scalar(s_junk[:, :], src_fn(1), sm[thr][:, :],
                                               None, op0=ALU.is_ge, op1=ALU.add,
                                               accum_out=sm["cnb"][:, :]),
                     [srcres, thr], ["junk", "cnb"])
                v_op(lambda v, dst=dst: v.tensor_add(dst, dst, sm["cnb"][:, :]),
                     [dstres, "cnb"], [dstres])

            for _ in range(12):
                mid_update()
                count2(lambda h: s_bis[:, h * 4096:(h + 1) * 4096], "bis", "mid",
                       sm["cn"][:, :], "cn")
                branch(False)
            v_op(lambda v: v.tensor_scalar(sm["t0"][:, :], sm["lo"][:, :], -1.0, None,
                                           op0=ALU.mult), ["lo"], ["t0"])
            v_op(lambda v: v.tensor_max(sm["t0"][:, :], sm["t0"][:, :], sm["lo"][:, :]),
                 ["t0", "lo"], ["t0"])
            v_op(lambda v: v.scalar_tensor_tensor(sm["lo"][:, :], sm["t0"][:, :], -2.0**-7,
                                                  sm["lo"][:, :], op0=ALU.mult, op1=ALU.add),
                 ["t0", "lo"], ["lo"])
            v_op(lambda v: v.tensor_scalar(sm["lo"][:, :], sm["lo"][:, :], 1e-30, None,
                                           op0=ALU.subtract), ["lo"], ["lo"])
            v_op(lambda v: v.tensor_scalar(sm["t1"][:, :], sm["hi"][:, :], -1.0, None,
                                           op0=ALU.mult), ["hi"], ["t1"])
            v_op(lambda v: v.tensor_max(sm["t1"][:, :], sm["t1"][:, :], sm["hi"][:, :]),
                 ["t1", "hi"], ["t1"])
            v_op(lambda v: v.scalar_tensor_tensor(sm["hi"][:, :], sm["t1"][:, :], 2.0**-7,
                                                  sm["hi"][:, :], op0=ALU.mult, op1=ALU.add),
                 ["t1", "hi"], ["hi"])
            v_op(lambda v: v.tensor_scalar(sm["hi"][:, :], sm["hi"][:, :], 1e-30, None,
                                           op0=ALU.add), ["hi"], ["hi"])
            count2(lambda h: zap[:, h * 4096:(h + 1) * 4096], zres, "hi",
                   sm["chi"][:, :], "chi")
            for _ in range(6):
                mid_update()
                count2(lambda h: zap[:, h * 4096:(h + 1) * 4096], zres, "mid",
                       sm["cn"][:, :], "cn")
                branch(True)
            for q in range(4):
                zq = zap[:, q * 2048:(q + 1) * 2048]
                v_op(lambda v, zq=zq: v.scalar_tensor_tensor(s_u[:, :], zq, sm["hi"][:, :],
                                                             zq, op0=ALU.is_lt,
                                                             op1=ALU.mult),
                     [zres, "hi"], ["ubuf"])
                R.op("v", lambda v: v.max(out=s_m8[:, :], in_=s_u[:, :]),
                     reads=["ubuf"], writes=["m8"])
                v_op(lambda v, q=q: v.tensor_copy(s_c64[:, q * 16:q * 16 + 8],
                                                  s_m8[:, :]), ["m8"], ["c64"])
                R.op("v", lambda v: v.match_replace(out=s_u[:, :], in_to_replace=s_m8[:, :],
                                                    in_values=s_u[:, :], imm_value=NEG),
                     reads=["ubuf", "m8"], writes=["ubuf"])
                R.op("v", lambda v: v.max(out=s_m8[:, :], in_=s_u[:, :]),
                     reads=["ubuf"], writes=["m8"])
                v_op(lambda v, q=q: v.tensor_copy(
                        s_c64[:, q * 16 + 8:q * 16 + 16], s_m8[:, :]),
                     ["m8"], ["c64"])
            R.op("v", lambda v: v.max(out=s_m16[:, :8], in_=s_c64[:, :]),
                 reads=["c64"], writes=["m16"])
            R.op("v", lambda v: v.match_replace(out=s_c64[:, :], in_to_replace=s_m16[:, :8],
                                                in_values=s_c64[:, :], imm_value=NEG),
                 reads=["c64", "m16"], writes=["c64"])
            R.op("v", lambda v: v.max(out=s_m16[:, 8:], in_=s_c64[:, :]),
                 reads=["c64", "m16"], writes=["m16"])
            v_op(lambda v: v.tensor_scalar(sm["r"][:, :], sm["chi"][:, :], -1.0, float(KTOP),
                                           op0=ALU.mult, op1=ALU.add), ["chi"], ["r"])
            v_op(lambda v: v.scalar_tensor_tensor(s_cand[:, :], s_iota[:, :], sm["r"][:, :],
                                                  s_m16[:, :], op0=ALU.is_equal,
                                                  op1=ALU.mult, accum_out=sm["vk"][:, :]),
                 ["c_iota", "r", "m16"], ["cand", "vk"])

        def pre_z(blk):
            tiles = (2 * blk, 2 * blk + 1)
            for tt, j in enumerate(tiles):
                dma("y", s_x1t[:, :], d_x1.ap()[j * 128:(j + 1) * 128, :],
                    reads=["x1t", f"x1_{j}"], writes=["x1t"])
                ln_tile(s_x1t[:, :], "x1t", s_x2[:, :], "x2", D,
                        save=(s_m3[:, j, :], f"m3_{j}"))
                transpose16(lambda c: s_x2[:, c * 128:(c + 1) * 128], "x2",
                            lambda c, tt=tt: s_x2T[:, c, tt * 128:(tt + 1) * 128], "x2T")
            for u in range(16):
                dma("y", s_tsn[:, :, :],
                    d_tsyn.ap()[:, :, u * 512:(u + 1) * 512].rearrange("c p u -> p c u"),
                    reads=["tsn", "TSYN"], writes=["tsn"])
                dma("y", s_brow[:1, :], d_bsr.ap()[:1, u * 512:(u + 1) * 512],
                    reads=["brow"], writes=["brow"])
                for tt in range(2):
                    pp, ppr = (p_a, "pA") if tt == 0 else (p_b, "pB")
                    mm(pp[:, :], s_1bf[:1, :], s_brow[:1, :],
                       ["onesbf", "brow"], ppr, True, False, mark=(tt == 1))
                    for c in range(DBLK):
                        mm(pp[:, :], s_x2T[:, c, tt * 128:(tt + 1) * 128], s_tsn[:, c, :],
                           ["x2T", "tsn"], ppr, False, c == DBLK - 1)
                    s_op(lambda s, pp=pp, tt=tt, u=u: s.activation(
                            s_z[tt][:, u * 512:(u + 1) * 512], pp[:, :], AF.Identity,
                            scale=s_nss[:, :]),
                         [ppr, "nss"], [f"z{tt}"])

        def topk_consume(tt, j, do_split):
            zap = s_z[tt][:, :]
            zres = f"z{tt}"
            if LITE[0] and j == 0:
                dma("g", dbg_t["dzl"].ap()[:, :], zap, reads=[zres])
            if dbg:
                dma("g", dbg_t["dz"].ap()[j * 128:(j + 1) * 128, :], zap, reads=[zres])
            topk_tile(zres, zap, jtap=j)
            if dbg or LITE[0]:
                v_op(lambda v: v.tensor_copy(s_c64[:, 0:1], sm["vk"][:, :]),
                     ["vk"], ["c64"])
                v_op(lambda v: v.tensor_copy(s_c64[:, 1:2], sm["chi"][:, :]),
                     ["chi"], ["c64"])
                v_op(lambda v: v.tensor_copy(s_c64[:, 2:3], sm["hi"][:, :]),
                     ["hi"], ["c64"])
                v_op(lambda v: v.tensor_copy(s_c64[:, 3:4], sm["r"][:, :]),
                     ["r"], ["c64"])
                dma("g", dbg_t["dvk"].ap()[j * 128:(j + 1) * 128, :], s_c64[:, 0:4],
                    reads=["c64"])
            
            v_op(lambda v, zap=zap: v.scalar_tensor_tensor(zap, zap, sm["vk"][:, :],
                                                            zap, op0=ALU.is_ge,
                                                            op1=ALU.mult),
                 [zres, "vk"], [zres])
            if dbg:
                dma("g", dbg_t["dhid"].ap()[j * 128:(j + 1) * 128, :], zap, reads=[zres])
            if do_split:
                split_tile(tt)

        def split_tile(tt):
            zap = s_z[tt][:, :]
            zres = f"z{tt}"
            v_op(lambda v, zap=zap: v.tensor_scalar(s_A[:, :], zap, 0.0, None,
                                                    op0=ALU.max),
                 [zres], ["Abuf"])
            v_op(lambda v, zap=zap: v.scalar_tensor_tensor(s_Bb[:, :], zap, 0.0,
                                                           s_A[:, :], op0=ALU.max,
                                                           op1=ALU.subtract),
                 [zres, "Abuf"], ["Bbuf"])

        def mm2_tile():
            for db in range(4):
                mm(p_mm[:, db * 512:(db + 1) * 512], s_1bf[:1, :],
                   s_borr[:1, db * 512:(db + 1) * 512],
                   ["onesbf", "borr"], "pMM", True, False, mark=(db == 3))
            for u in range(UCH):
                pbT = u % 2
                dma("y", s_ton[pbT][:, :], d_tout.ap()[u, :, :],
                    reads=[f"ton{pbT}", "TOUT"], writes=[f"ton{pbT}"])
                tr(p_t[:, 0:128], s_A[:, u * 128:(u + 1) * 128], ["Abuf"], mark=False)
                tr(p_t[:, 128:256], s_Bb[:, u * 128:(u + 1) * 128], ["Bbuf"], mark=True)
                s_op(lambda s, pbT=pbT: s.activation(s_hiT[pbT][:, :], p_t[:, 0:128],
                                                     AF.Identity),
                     ["pT"], [f"hiT{pbT}"])
                s_op(lambda s, pbT=pbT: s.activation(s_loT[pbT][:, :], p_t[:, 128:256],
                                                     AF.Identity),
                     ["pT"], [f"loT{pbT}"])
                for db in range(4):
                    mm(p_mm[:, db * 512:(db + 1) * 512], s_hiT[pbT][:, :],
                       s_ton[pbT][:, db * 512:(db + 1) * 512],
                       [f"hiT{pbT}", f"ton{pbT}"], "pMM", False, False, mark=False)
                    mm(p_mm[:, db * 512:(db + 1) * 512], s_loT[pbT][:, :],
                       s_ton[pbT][:, db * 512:(db + 1) * 512],
                       [f"loT{pbT}", f"ton{pbT}"], "pMM", False, u == UCH - 1,
                       mark=(db == 3))

        def ln_chain(j):
            s_op(lambda s: s.activation(s_out[:, :], p_mm[:, :], AF.Identity,
                                        scale=s_nso[:, :]),
                 ["pMM", "nso"], ["outb"])
            if LITE[0] and j == 0:
                dma("g", dbg_t["dfl"].ap()[:, :], s_out[:, :], reads=["outb"])
            if dbg:
                dma("g", dbg_t["dffn"].ap()[j * 128:(j + 1) * 128, :], s_out[:, :],
                    reads=["outb"])
            dma("y", s_x1t[:, :], d_x1.ap()[j * 128:(j + 1) * 128, :],
                reads=["x1t", f"x1_{j}"], writes=["x1t"])
            v_op(lambda v, j=j: v.tensor_scalar(s_x2[:, :], s_x1t[:, :],
                                                s_m3[:, j, 0:1], s_m3[:, j, 1:2],
                                                op0=ALU.subtract, op1=ALU.mult),
                 ["x1t", f"m3_{j}"], ["x2"])
            v_op(lambda v: v.tensor_add(s_out[:, :], s_out[:, :], s_x2[:, :]),
                 ["outb", "x2"], ["outb"])
            ln_tile(s_out[:, :], "outb", s_sqscr[:, :], "sqscr")
            v_op(lambda v: v.tensor_mul(s_out[:, :], s_sqscr[:, :], s_gg[:, :]),
                 ["sqscr", "ggam"], ["outb"])
            v_op(lambda v: v.tensor_add(s_out[:, :], s_out[:, :], s_gbt[:, :]),
                 ["outb", "gbet"], ["outb"])
            ln_tile(s_out[:, :], "outb", s_sqscr[:, :], "sqscr")
            v_op(lambda v: v.tensor_add(s_out[:, :], s_sqscr[:, :], s_x1t[:, :]),
                 ["sqscr", "x1t"], ["outb"])
            dma("g", d_y.ap()[j * 128:(j + 1) * 128, :], s_out[:, :],
                reads=["outb"], writes=[f"y{j}"])

        PIPE = False
        pre_z(0)
        pending_ln = None
        for blk in range(4):
            t0j, t1j = 2 * blk, 2 * blk + 1
            if not PIPE and blk > 0:
                pre_z(blk)
            if pending_ln is not None:
                ln_chain(pending_ln)
            topk_consume(0, t0j, do_split=True)
            topk_consume(1, t1j, do_split=False)
            mm2_tile()
            split_tile(1)
            ln_chain(t0j)
            if PIPE and blk < 3:
                pre_z(blk + 1)
            mm2_tile()
            if PIPE:
                pending_ln = t1j
            else:
                ln_chain(t1j)
        if pending_ln is not None:
            ln_chain(pending_ln)
        if LITE[0]:
            v_op(lambda v: v.tensor_copy(s_out[:, :], s_gg[:, :]), ["ggam"], ["outb"])
            dma("g", dbg_t["dcst"].ap()[:, 0:D], s_out[:, :], reads=["outb"])
            v_op(lambda v: v.tensor_copy(s_out[:, :], s_gbt[:, :]), ["gbet"], ["outb"])
            dma("g", dbg_t["dcst"].ap()[:, D:2 * D], s_out[:, :], reads=["outb"])
            v_op(lambda v: v.tensor_copy(s_out[:, 0:1], s_nss[:, :]), ["nss"], ["outb"])
            v_op(lambda v: v.tensor_copy(s_out[:, 1:2], s_nso[:, :]), ["nso"], ["outb"])
            dma("g", dbg_t["dcst"].ap()[:, 2 * D:2 * D + 2], s_out[:, 0:2],
                reads=["outb"])
        R.barrier()

    if dbg_lite:
        dma("g", dbg_t["dx1f"].ap()[:, :], d_x1.ap()[:, :], reads=["x1_0"])
        for uc in range(UCH):
            dma("g", dbg_t["dtout"].ap()[uc, :, :], d_tout.ap()[uc, :, :],
                reads=["TOUT"])
        R.barrier()
    with nc.Block() as block:
        R.finalize(block)
    return nc, dbg_t


def kernel(L0_attn_x, L0_attn_Wq, L0_attn_bq, L0_attn_Wk, L0_attn_bk,
           L0_attn_Wv, L0_attn_bv, L0_attn_Wo, L0_ffn_W_syn, L0_ffn_b_syn,
           L0_ffn_W_out, L0_ffn_b_out, L0_ffn_gamma, L0_ffn_beta, _dbg=False,
           _dbg_lite=False):
    x = np.asarray(L0_attn_x)
    in_maps = []
    with contextlib.ExitStack() as ctx:
        nc, dbg_t = build_program(ctx, dbg=_dbg, dbg_lite=_dbg_lite)
        for core in range(8):
            b, r = core // 4, core % 4
            tb = host_tables(r)
            xb = np.ascontiguousarray(x[b])
            in_maps.append({
                "xf": xb,
                "xo": np.ascontiguousarray(xb[tb["own_rows"]]),
                "wq": np.asarray(L0_attn_Wq), "wk": np.asarray(L0_attn_Wk),
                "wv": np.asarray(L0_attn_Wv), "wo": np.asarray(L0_attn_Wo),
                "bqc": np.ascontiguousarray(np.asarray(L0_attn_bq).reshape(16, 128).T),
                "bkc": np.ascontiguousarray(np.asarray(L0_attn_bk).reshape(4, 128).T),
                "bvr": np.asarray(L0_attn_bv).reshape(1, 512),
                "wsyn": np.asarray(L0_ffn_W_syn), "wout": np.asarray(L0_ffn_W_out),
                "bsyn": np.asarray(L0_ffn_b_syn).reshape(1, NFF),
                "bout": np.asarray(L0_ffn_b_out).reshape(1, D),
                "gam": np.asarray(L0_ffn_gamma).reshape(1, D),
                "bet": np.asarray(L0_ffn_beta).reshape(1, D),
                "cfhi": tb["fhi"], "cflo": tb["flo"], "ctf": tb["tf"], "cto": tb["to"],
                "cksf": tb["ksf"], "ckcf": tb["kcf"], "ckso": tb["kso"],
                "ckco": tb["kco"], "ccm": tb["cmask"], "cid": tb["ident"],
                "cones": tb["ones"], "ciota": tb["iota16"],
            })
        res = run_bass_kernel_spmd(nc, in_maps, core_ids=list(range(8)))
    y = np.empty((B, T, D), dtype=np.float32)
    for core in range(8):
        b, r = core // 4, core % 4
        tb = host_tables(r)
        y[b][tb["own_rows"]] = res.results[core]["y"]
    if _dbg or _dbg_lite:
        return y, res
    return y
